# version 08 (final)
# speedup vs baseline: 1.4242x; 1.0253x over previous
import numpy as np
import concourse.bass as bass
import concourse.mybir as mybir
from concourse.bass_utils import run_bass_kernel_spmd

F32 = mybir.dt.float32
BF16 = mybir.dt.bfloat16
I32 = mybir.dt.int32
U32 = mybir.dt.uint32
AF = mybir.ActivationFunctionType
ALU = mybir.AluOpType
AX = mybir.AxisListType

DEBUG = False
B, S, D = 4, 4096, 1024
T = B * S
TOWN = 2048
NT = TOWN // 128
NTB = S // 128
NF = D // 128
GT = T // 128

L0_E, L0_CAP, CAPACITY = 8, 384, 2048
L0_NSLOT = L0_E * L0_CAP
L1_E, L1_CAP = 16, 320
L1_NSLOT = L1_E * L1_CAP
FF = 4096
NFF = FF // 128

ENGS = ("sync", "scalar", "vector", "tensor", "gpsimd")
ESEM = {"scalar": "cA", "vector": "cV", "tensor": "cP", "gpsimd": "cG", "sync": None}


GLOBAL_SEMS = ("cA", "cV", "cP", "cG", "bar")


class P:
    def __init__(self, nc):
        self.nc = nc
        self.q = {e: [] for e in ENGS}
        self._sems = {}
        self.cnt = {}
        self.ns = ""
        self._stack = []
        self._lastbar = {}
        self.pending = {}

    def _k(self, name):
        if name in GLOBAL_SEMS or "." in name:
            return name
        return self.ns + name

    def n(self, name):
        return self.cnt.get(self._k(name), 0)

    def sem(self, name):
        if name not in self._sems:
            self._sems[name] = self.nc.semaphore(name).__enter__()
            self.cnt[name] = 0
        return self._sems[name]

    def sb(self, name, shape, dtype):
        cm = self.nc.sbuf_tensor(self.ns + name, shape, dtype)
        h = cm.__enter__()
        self._stack.append(cm)
        return h

    def ps(self, name, shape, dtype):
        cm = self.nc.psum_tensor(self.ns + name, shape, dtype)
        h = cm.__enter__()
        self._stack.append(cm)
        return h

    def mark(self):
        return len(self._stack)

    def release(self, m):
        while len(self._stack) > m:
            self._stack.pop().__exit__(None, None, None)

    def dram(self, name, shape, dtype, kind="Internal", addr_space="Local"):
        nm = name if kind.startswith("External") else self.ns + name
        return self.nc.dram_tensor(nm, shape, dtype, kind=kind,
                                   addr_space=addr_space).ap()

    def emit(self, eng, build, waits=(), inc="auto", inc_by=1):
        if inc == "auto":
            inc = ESEM[eng]
        if inc:
            inc = self._k(inc)
        sem_h = self.sem(inc) if inc else None
        wmax = {}
        pend = self.pending.pop(eng, None)
        for wt in list(waits) + (pend or []):
            if wt is None:
                continue
            sname, c = wt
            if sname is not None and c > 0:
                wmax[self._k(sname)] = max(wmax.get(self._k(sname), 0), c)
        wl = [(self.sem(sname), c) for sname, c in wmax.items()]

        def thunk(e, build=build, wl=wl, sem_h=sem_h, inc_by=inc_by):
            for h, c in wl:
                e.wait_ge(h, c)
            ins = build(e)
            if sem_h is not None:
                if ins is None:
                    ins = e.nop()
                ins.then_inc(sem_h, inc_by)

        self.q[eng].append(thunk)
        if inc:
            self.cnt[inc] += inc_by
            return (inc, self.cnt[inc])
        return None

    def dma(self, eng, out, in_, sem, waits=(), **kw):
        return self.emit(eng, lambda e: e.dma_start(out=out, in_=in_, **kw),
                         waits=waits, inc=sem, inc_by=16)

    def tok(self, sem):
        k = self._k(sem)
        return (k, self.cnt.get(k, 0))

    def fence(self, engs=("vector", "scalar", "tensor", "gpsimd"), dma_sems=()):
        toks = [self.tok(ESEM[e]) for e in engs] + [self.tok(s) for s in dma_sems]
        for e in engs:
            w = [t for t in toks if t[0] != ESEM[e] and t[1] > 0]
            if w:
                self.emit(e, lambda _e: None, waits=w, inc=None)

    def phase_end(self):
        toks = [(k, c) for k, c in self.cnt.items()
                if c > self._lastbar.get(k, 0) and k != "bar"]
        for e in ("vector", "scalar", "tensor", "gpsimd"):
            self.emit(e, lambda _e: None, waits=toks, inc="bar", inc_by=1)
        self._lastbar = dict(self.cnt)
        bt = self.tok("bar")
        self.pending = {e: [bt] for e in ENGS}

    def finalize(self, out_sems=()):
        for s in out_sems:
            k = self._k(s)
            self.emit("gpsimd", lambda g, k=k, n=self.cnt.get(k, 0):
                      g.wait_ge(self._sems[k], n), inc=None)
        with self.nc.Block() as block:
            for eng in ENGS:
                thunks = self.q[eng]
                if not thunks:
                    continue

                def run(e, th=tuple(thunks)):
                    for t in th:
                        t(e)
                getattr(block, eng)(run)
        return self.nc


def make_ident(p, dtype=BF16, name="ident"):
    ident = p.sb(name, [128, 128], dtype)
    t1 = p.emit("vector", lambda v: v.memset(ident[:, :], 1.0))
    t2 = p.emit("gpsimd", lambda g: g.affine_select(
        ident[:, :], ident[:, :], [[-1, 128]], ALU.is_equal, 0.0,
        channel_multiplier=1), waits=[t1])
    return ident, t2


def ln_tile(p, xn_out, xt, st, sq, waits, eps=1e-5):
    Dw = xt.shape[1]
    t_sum = p.emit("vector", lambda v: v.reduce_sum(st[:, 0:1], xt[:, :], axis=AX.X),
                   waits=waits)
    t_sq = p.emit("scalar", lambda s: s.activation(sq[:, :], xt[:, :], AF.Square),
                  waits=waits)
    t_msq = p.emit("vector", lambda v: v.reduce_sum(st[:, 1:2], sq[:, :], axis=AX.X),
                   waits=[t_sq])
    t_m = p.emit("vector", lambda v: v.tensor_scalar(st[:, 2:3], st[:, 0:1],
                                                     -1.0 / Dw, None, ALU.mult),
                 waits=[t_sum])
    t_v1 = p.emit("vector", lambda v: v.tensor_tensor(st[:, 3:4], st[:, 2:3],
                                                      st[:, 2:3], ALU.mult),
                  waits=[t_m])
    t_v2 = p.emit("vector", lambda v: v.tensor_scalar(st[:, 4:5], st[:, 1:2],
                                                      1.0 / Dw, st[:, 3:4],
                                                      ALU.mult, ALU.subtract),
                  waits=[t_v1, t_msq])
    t_v3 = p.emit("vector", lambda v: v.tensor_scalar(st[:, 4:5], st[:, 4:5],
                                                      eps, None, ALU.add),
                  waits=[t_v2])
    t_sd = p.emit("scalar", lambda s: s.activation(st[:, 5:6], st[:, 4:5], AF.Sqrt),
                  waits=[t_v3])
    t_r = p.emit("vector", lambda v: v.reciprocal(st[:, 6:7], st[:, 5:6]),
                 waits=[t_sd])
    t_xn = p.emit("vector", lambda v: v.tensor_scalar(
        xn_out[:, :], xt[:, :], st[:, 2:3], st[:, 6:7], ALU.add, ALU.mult),
        waits=[t_r])
    return t_xn


def transpose_pass(p, dst_sb, src_sb, ident, pts, waits, n128, t_id,
                   src_col0=0, dst_col0=0, evac="vector"):
    toks = []
    t_ev = {0: None, 1: None}
    for f in range(n128):
        bk = f % 2
        w = list(waits) + [t_id] + ([t_ev[bk]] if t_ev[bk] else [])
        t_tr = p.emit("tensor", lambda te, f=f, bk=bk: te.transpose(
            pts[bk][:, :], src_sb[:, src_col0 + f * 128:src_col0 + (f + 1) * 128],
            ident[:, :]), waits=w, inc="cP")
        def ev(e, f=f, bk=bk):
            dst = dst_sb[:, dst_col0 + f * 128:dst_col0 + (f + 1) * 128]
            if evac == "scalar":
                return e.copy(dst, pts[bk][:, :])
            return e.tensor_copy(dst, pts[bk][:, :])
        t_ev[bk] = p.emit(evac, ev, waits=[t_tr])
        toks.append(t_ev[bk])
    return toks



def emit_route0(p, io):
    x = io["x"]
    rw = io["rw"]
    o_idx = io["o_idx"]
    o_gate = io["o_gate"]

    xt = [p.sb(f"xt{i}", [128, D], F32) for i in range(2)]
    sq = p.sb("sq", [128, D], F32)
    st = p.sb("st", [128, 8], F32)
    xn = p.sb("xn", [128, D], F32)
    xnT = p.sb("xnT", [128, D], F32)
    rwsb = p.sb("rwsb", [128, NF * L0_E], F32)
    lg = p.sb("lg", [128, L0_E], F32)
    mx = p.sb("mx", [128, 8], F32)
    mi = p.sb("mi", [128, 8], U32)
    outc = p.sb("outc", [128, 2], F32)
    pts = [p.ps("pt0", [128, 128], F32), p.ps("pt1", [128, 128], F32)]
    pl = p.ps("pl", [128, L0_E], F32)

    ident, t_id = make_ident(p, F32, "identf")
    t_rw = p.dma("gpsimd", rwsb[:, :].rearrange("p (c e) -> p c e", c=NF),
                 rw.rearrange("(c p) e -> p c e", p=128), sem="dW")

    for t in range(NT):
        bk = t % 2
        if t >= 1:
            p.fence()
        t_x = p.dma("sync", xt[bk][:, :], x[t * 128:(t + 1) * 128, :], sem="dX",
                    waits=[p.tok("cV"), p.tok("cA")])
        t_xn = ln_tile(p, xn, xt[bk], st, sq, waits=[t_x])
        t_tr = transpose_pass(p, xnT, xn, ident, pts, [t_xn], NF, t_id)

        def mm(te):
            ins = None
            for f in range(NF):
                ins = te.matmul(pl[:, :], xnT[:, f * 128:(f + 1) * 128],
                                rwsb[:, f * L0_E:(f + 1) * L0_E],
                                start=(f == 0), stop=(f == NF - 1))
            return ins
        t_mm = p.emit("tensor", mm, waits=[t_rw] + t_tr, inc="cP")
        t_l = p.emit("vector", lambda v: v.tensor_copy(lg[:, :], pl[:, :]),
                     waits=[t_mm])
        t_m0 = p.emit("vector", lambda v: v.memset(mx[:, :], -1e30))
        t_m1 = p.emit("vector", lambda v: v.reduce_max(mx[:, 0:1], lg[:, :],
                                                       axis=AX.X),
                      waits=[t_l, t_m0])
        t_ix = p.emit("vector", lambda v: v.max_index(mi[:, :], mx[:, :], lg[:, :]),
                      waits=[t_m1])
        t_nm = p.emit("vector", lambda v: v.tensor_scalar(mx[:, 1:2], mx[:, 0:1],
                                                          -1.0, None, ALU.mult),
                      waits=[t_m1, t_ix])
        t_e = p.emit("scalar", lambda s: s.activation(lg[:, :], lg[:, :], AF.Exp,
                                                      bias=mx[:, 1:2]),
                     waits=[t_nm, t_ix])
        t_s = p.emit("vector", lambda v: v.reduce_sum(mx[:, 2:3], lg[:, :],
                                                      axis=AX.X), waits=[t_e])
        t_g = p.emit("vector", lambda v: v.reciprocal(outc[:, 1:2], mx[:, 2:3]),
                     waits=[t_s, p.tok("dO")])
        t_i2 = p.emit("vector", lambda v: v.tensor_copy(outc[:, 0:1], mi[:, 0:1]),
                      waits=[t_ix])
        p.dma("sync", o_idx[t * 128:(t + 1) * 128, :], outc[:, 0:1],
              sem="dO", waits=[t_i2, t_g])
        p.dma("sync", o_gate[t * 128:(t + 1) * 128, :], outc[:, 1:2],
              sem="dO", waits=[t_i2, t_g])


def build_route0():
    nc = bass.Bass("TRN2", target_bir_lowering=False)
    p = P(nc)
    io = {}
    io["x"] = p.dram("x", [TOWN, D], F32, kind="ExternalInput")
    io["rw"] = p.dram("rw", [D, L0_E], F32, kind="ExternalInput")
    io["o_idx"] = p.dram("o_idx", [TOWN, 1], F32, kind="ExternalOutput")
    io["o_gate"] = p.dram("o_gate", [TOWN, 1], F32, kind="ExternalOutput")
    emit_route0(p, io)
    return p.finalize(out_sems=["dO"])


def emit_ffn0(p, io):
    x = io["x"]
    idxg = io["idxg"]
    oidx = io["oidx"]
    ogate = io["ogate"]
    owntiles = io["owntiles"]
    w1 = io["w1"]
    b1 = io["b1"]
    w2 = io["w2"]
    b2 = io["b2"]
    xo = io["xo"]

    vst = p.dram("vst", [L0_NSLOT, D + 4], BF16)
    ybuf = p.dram("ybuf", [L0_NSLOT, D], BF16)

    CH = 4096
    NCH = T // CH
    xt = [p.sb(f"xt{i}", [128, D], F32) for i in range(2)]
    sq = p.sb("sq", [128, D], F32)
    st = p.sb("st", [128, 8], F32)
    xn = p.sb("xn", [128, D + 4], BF16)
    idp = p.sb("idp", [128, 1], BF16)
    idcols = p.sb("idcols", [128, L0_NSLOT // 128], F32)
    yld = [p.sb(f"yld{i}", [128, D], BF16) for i in range(2)]
    Psm = [p.sb(f"Psm{i}", [128, 128], BF16) for i in range(2)]
    tIot = p.sb("tIot", [128, 128], F32)
    adjc = p.sb("adjc", [128, 2], F32)
    triUf = p.sb("triUf", [128, 128], F32)
    ones1f = p.sb("ones1f", [1, 128], F32)
    onec128 = p.sb("onec128", [128, 1], F32)
    eio = p.sb("eio", [128, L0_E], F32)
    emask = p.sb("emask", [128, L0_E], F32)
    kem = p.sb("kem", [128, L0_E], F32)
    lrank = p.sb("lrank", [128, L0_E], F32)
    lcarry = p.sb("lcarry", [1, L0_E], F32)
    cols = p.sb("cols", [128, 8], F32)
    icol = p.sb("icol", [128, 1], F32)
    gcol = p.sb("gcol", [128, 1], F32)
    desti = p.sb("desti", [128, 1], I32)
    ghalf = p.sb("ghalf", [128, NT], F32)
    posown = p.sb("posown", [128, NT], F32)
    poscols = p.sb("poscols", [128, GT], F32)
    pcT = p.sb("pcT", [128, GT], F32)
    selt = p.sb("selt", [128, NT], F32)
    possel = p.sb("possel", [NT, 128], F32)
    otrow = p.sb("otrow", [1, NT], F32)
    gio = p.sb("gio", [128, 1], F32)
    identf, t_idf = make_ident(p, F32, "identf")
    identb, t_idb = make_ident(p, BF16, "identb")
    w1sb = [p.sb(f"w1sb{i}", [128, NF * (FF // 4)], BF16) for i in range(2)]
    w2sb = [p.sb(f"w2sb{i}", [128, NFF * 512], BF16) for i in range(2)]
    b1sb = p.sb("b1sb", [128, NFF], F32)
    b1r = p.sb("b1r", [32, 128], F32)
    b2row = p.sb("b2row", [1, D], F32)
    b2bc = p.sb("b2bc", [128, D], F32)
    xg = p.sb("xg", [128, D], BF16)
    xgT = p.sb("xgT", [128, NF * L0_CAP], BF16)
    hT = p.sb("hT", [128, NFF * L0_CAP], BF16)
    ysb = [p.sb(f"ysb{i}", [128, D], BF16) for i in range(3)]
    xot = p.sb("xot", [128, D], F32)
    idxr = p.sb("idxr", [8, CH], BF16)
    emTc = p.sb("emTc", [8, CH], BF16)
    inclc = p.sb("inclc", [8, CH], F32)
    rcar = p.sb("rcar", [8, 1], F32)
    ecol = p.sb("ecol", [8, 1], F32)
    onescf = p.sb("onescf", [8, 1], F32)
    pbk = [p.ps(f"pbk{i}", [128, 512], F32) for i in range(6)]
    ptb = [p.ps("ptb0", [128, 128], BF16), p.ps("ptb1", [128, 128], BF16)]
    ppos = [pbk[0][:, 0:1], pbk[1][:, 0:1]]
    plincl = pbk[2][:, 0:L0_E]
    pltot = pbk[3][0:1, 256:256 + L0_E]
    hps = [pbk[4][:, 0:L0_CAP], pbk[5][:, 0:L0_CAP]]
    yps = [pbk[0][:, :], pbk[1][:, :]]
    pb2 = pbk[3][:, :]

    t_tri = p.emit("vector", lambda v: v.memset(triUf[:, :], 1.0))
    t_tri = p.emit("gpsimd", lambda g: g.affine_select(
        triUf[:, :], triUf[:, :], [[1, 128]], ALU.is_ge, 0.0,
        channel_multiplier=-1), waits=[t_tri])
    t_1f = p.emit("vector", lambda v: v.memset(ones1f[:, :], 1.0))
    t_1c = p.emit("vector", lambda v: v.memset(onec128[:, :], 1.0))
    t_ei = p.emit("gpsimd", lambda g: g.iota(eio[:, :], [[1, L0_E]], base=0,
                                             channel_multiplier=0,
                                             allow_small_or_imprecise_dtypes=True))
    t_lc = p.emit("vector", lambda v: v.memset(lcarry[:, :], 0.0))
    t_zx = p.emit("vector", lambda v: v.memset(xn[:, :], 0.0))
    t_vz = None
    for zt in range(L0_NSLOT // 128):
        t_vz = p.dma("sync", vst[zt * 128:(zt + 1) * 128, :], xn[:, :],
                     sem="dVz", waits=[t_zx])
    n_vz = p.n("dVz")

    t_ec = p.emit("gpsimd", lambda g: g.iota(ecol[:, :], [[1, 1]], base=0,
                                             channel_multiplier=1,
                                             allow_small_or_imprecise_dtypes=True))
    t_oc = p.emit("vector", lambda v: v.memset(onescf[:, :], 1.0))
    t_rc = p.emit("vector", lambda v: v.memset(rcar[:, :], 0.0))
    t_pc = {0: None, 1: None}
    for cch in range(NCH):
        t_idx = []
        for pp in range(8):
            t_idx.append(p.dma(
                "gpsimd", idxr[pp:pp + 1, :],
                idxg[cch * CH:(cch + 1) * CH, :].rearrange(
                    "(o t) one -> o (t one)", o=1), sem="dR",
                waits=[p.tok("cV")]))
        t_em = p.emit("vector", lambda v: v.tensor_scalar(
            emTc[:, :], idxr[:, :], ecol[:, :], None, ALU.is_equal),
            waits=t_idx + [t_ec])
        t_sc = p.emit("vector", lambda v: v.tensor_tensor_scan(
            inclc[:, :], emTc[:, :], emTc[:, :],
            rcar[:, 0:1], ALU.add, ALU.bypass), waits=[t_em, t_rc])
        t_rc = p.emit("vector", lambda v: v.tensor_copy(
            rcar[:, :], inclc[:, CH - 1:CH]), waits=[t_sc])
        t_pm = p.emit("vector", lambda v: v.tensor_tensor(
            inclc[:, :], inclc[:, :], emTc[:, :], ALU.mult), waits=[t_rc])
        for gl in range(CH // 128):
            g = cch * (CH // 128) + gl
            bk = gl % 2
            t_mmp = p.emit("tensor", lambda te, gl=gl, bk=bk: te.matmul(
                ppos[bk], inclc[:, gl * 128:(gl + 1) * 128], onescf[:, :],
                start=True, stop=True),
                waits=[t_pm, t_oc] + ([t_pc[bk]] if t_pc[bk] else []), inc="cP")
            t_pc[bk] = p.emit("vector", lambda v, g=g, bk=bk: v.tensor_copy(
                poscols[:, g:g + 1], ppos[bk]), waits=[t_mmp])
    t_pcT = []
    for gq in range(GT // 128):
        t_tp = p.emit("tensor", lambda te, gq=gq: te.transpose(
            pbk[2][:, 0:128], poscols[:, gq * 128:(gq + 1) * 128], identf[:, :]),
            waits=[t_pc[0], t_pc[1], t_idf] + t_pcT, inc="cP")
        t_pcT.append(p.emit("vector", lambda v, gq=gq: v.tensor_copy(
            pcT[:, gq * 128:(gq + 1) * 128], pbk[2][:, 0:128]), waits=[t_tp]))
    t_ot = p.dma("sync", otrow[:, :], owntiles[:, :], sem="dOt")
    t_gi = p.emit("gpsimd", lambda g: g.iota(gio[:, :], [[1, 1]], base=0,
                                             channel_multiplier=1,
                                             allow_small_or_imprecise_dtypes=True))
    t_sb = p.emit("tensor", lambda te: te.matmul(
        pbk[2][:, 0:NT], ones1f[:, :], otrow[:, :], start=True, stop=True),
        waits=[t_ot, t_1f] + t_pcT, inc="cP")
    t_sb = p.emit("vector", lambda v: v.tensor_copy(selt[:, :], pbk[2][:, 0:NT]),
                  waits=[t_sb])
    t_sb = p.emit("vector", lambda v: v.tensor_scalar(
        selt[:, :], selt[:, :], gio[:, 0:1], None, ALU.is_equal),
        waits=[t_sb, t_gi])
    t_ps = p.emit("tensor", lambda te: te.matmul(
        pbk[3][0:NT, 0:128], selt[:, :], pcT[:, :], start=True, stop=True),
        waits=[t_sb, p.tok("cV")], inc="cP")
    t_ps = p.emit("vector", lambda v: v.tensor_copy(possel[:, :],
                                                    pbk[3][0:NT, 0:128]),
                  waits=[t_ps])
    t_po = p.emit("tensor", lambda te: te.transpose(
        pbk[2][:, 0:NT], possel[:, :], identf[0:NT, 0:NT]),
        waits=[t_ps], inc="cP")
    t_po = p.emit("vector", lambda v: v.tensor_copy(posown[:, :],
                                                    pbk[2][:, 0:NT]),
                  waits=[t_po])

    t_idp = p.emit("gpsimd", lambda g: g.iota(
        idp[:, :], [[1, 1]], base=0, channel_multiplier=1,
        allow_small_or_imprecise_dtypes=True))
    for t in range(NT):
        bk = t % 2
        t_x = p.dma("sync", xt[bk][:, :], x[t * 128:(t + 1) * 128, :], sem="dX",
                    waits=[p.tok("cV"), p.tok("cA")])
        t_xn = ln_tile(p, xn[:, 0:D], xt[bk], st, sq,
                       waits=[t_x, p.tok("dScat"), ("dVz", n_vz)])
        t_id1 = p.emit("vector", lambda v, t=t: v.memset(
            xn[:, D:D + 1], float(t * 128)), waits=[p.tok("dScat")])
        t_id2 = p.emit("vector", lambda v: v.tensor_copy(
            xn[:, D + 1:D + 2], idp[:, :]), waits=[t_id1, t_idp])
        t_id3 = p.emit("vector", lambda v: v.memset(
            xn[:, D + 2:D + 4], 1.0), waits=[t_id2])
        t_ic = p.dma("sync", icol[:, :], oidx[t * 128:(t + 1) * 128, :],
                     sem="dMi", waits=[p.tok("cV")])
        t_gc = p.dma("sync", gcol[:, :], ogate[t * 128:(t + 1) * 128, :],
                     sem="dMg", waits=[p.tok("cV")])
        t_c1 = p.emit("vector", lambda v, t=t: v.tensor_scalar(
            cols[:, 1:2], posown[:, t:t + 1], -1.0, None, ALU.add),
            waits=[t_po])
        t_c2 = p.emit("vector", lambda v: v.tensor_scalar(
            cols[:, 2:3], cols[:, 1:2], float(CAPACITY), None, ALU.is_ge),
            waits=[t_c1])
        t_em2 = p.emit("vector", lambda v: v.tensor_scalar(
            emask[:, :], eio[:, :], icol[:, :], None, ALU.is_equal),
            waits=[t_ic, t_ei])
        t_kp = p.emit("vector", lambda v: v.tensor_scalar(
            cols[:, 3:4], cols[:, 2:3], -1.0, 1.0, ALU.mult, ALU.add),
            waits=[t_c2])
        t_km = p.emit("vector", lambda v: v.tensor_scalar(
            kem[:, :], emask[:, :], cols[:, 3:4], None, ALU.mult),
            waits=[t_em2, t_kp])
        t_li = p.emit("tensor", lambda te: te.matmul(
            plincl, triUf[:, :], kem[:, :], start=True, stop=False),
            waits=[t_km, t_tri, t_lc, t_1f], inc="cP")
        t_li = p.emit("tensor", lambda te: te.matmul(
            plincl, ones1f[:, :], lcarry[:, :], start=False, stop=True),
            inc="cP", waits=[t_li])
        t_lr = p.emit("vector", lambda v: v.tensor_copy(lrank[:, :], plincl),
                      waits=[t_li])
        t_lt = p.emit("tensor", lambda te: te.matmul(
            pltot, onec128[:, :], kem[:, :], start=True, stop=True),
            waits=[t_km, t_1c, t_lr], inc="cP")
        t_lc = p.emit("vector", lambda v: v.tensor_tensor(
            lcarry[:, :], lcarry[:, :], pltot, ALU.add), waits=[t_lt, t_lc])
        t_krm = p.emit("vector", lambda v: v.tensor_tensor(
            lrank[:, :], lrank[:, :], kem[:, :], ALU.mult), waits=[t_lr])
        t_lrt = p.emit("vector", lambda v: v.reduce_sum(
            cols[:, 4:5], lrank[:, :], axis=AX.X), waits=[t_krm])
        t_d0 = p.emit("vector", lambda v: v.tensor_scalar(
            cols[:, 5:6], icol[:, :], float(L0_CAP), -1.0, ALU.mult, ALU.add),
            waits=[t_ic])
        t_d1 = p.emit("vector", lambda v: v.tensor_tensor(
            cols[:, 5:6], cols[:, 5:6], cols[:, 4:5], ALU.add),
            waits=[t_d0, t_lrt])
        t_ov = p.emit("vector", lambda v: v.tensor_scalar(
            cols[:, 6:7], cols[:, 4:5], float(L0_CAP), None, ALU.is_gt),
            waits=[t_lrt])
        t_d2 = p.emit("vector", lambda v: v.scalar_tensor_tensor(
            cols[:, 5:6], cols[:, 6:7], 1.0e9, cols[:, 5:6], ALU.mult, ALU.add),
            waits=[t_ov, t_d1])
        t_d3 = p.emit("vector", lambda v: v.scalar_tensor_tensor(
            cols[:, 5:6], cols[:, 2:3], 1.0e9, cols[:, 5:6], ALU.mult, ALU.add),
            waits=[t_d2])
        t_di = p.emit("vector", lambda v: v.tensor_copy(desti[:, :],
                                                        cols[:, 5:6]),
                      waits=[t_d3, p.tok("dScat")])
        p.emit("vector", lambda v, t=t: v.tensor_scalar(
            ghalf[:, t:t + 1], gcol[:, :], 0.5, None, ALU.mult), waits=[t_gc])
        p.emit("gpsimd", lambda g: g.indirect_dma_start(
            vst[:, :], bass.IndirectOffsetOnAxis(desti[:, :], 0),
            xn[:, :], None, bounds_check=L0_NSLOT - 1, oob_is_err=False),
            waits=[t_di, t_xn, t_id3, ("dVz", n_vz)], inc="dScat", inc_by=16)

    n_scat = p.n("dScat")
    tr_toks = []
    t_m1_prev = None
    t_m2_prev = None
    t_relu_prev = None
    t_bb_prev = None
    t_b1t_prev = None
    w1_rd, w1_ld, w2_rd, w2_ld = {}, {}, {}, {}
    for e in range(L0_E):
        t_b1r = p.dma("sync", b1r[:, :],
                      b1[e, :].rearrange("(c p) -> c p", p=128), sem="dB1",
                      waits=[t_b1t_prev, ("dScat", n_scat)])
        t_b1t = p.emit("tensor", lambda te: te.transpose(
            pbk[2][:, 0:32], b1r[:, :], identf[0:32, 0:32]),
            waits=[t_b1r, t_idf, p.tok("cV"), p.tok("cA")], inc="cP")
        t_b1t_prev = t_b1t
        t_b1 = p.emit("vector", lambda v: v.tensor_copy(
            b1sb[:, :], pbk[2][:, 0:NFF]), waits=[t_b1t, t_relu_prev])
        t_b2 = p.dma("sync", b2row[:, :], b2[e:e + 1, :], sem="dB2",
                     waits=[t_bb_prev])
        t_bb = p.emit("tensor", lambda te: te.matmul(
            pb2, ones1f[:, :], b2row[:, 0:512], start=True, stop=True),
            waits=[t_b2, t_1f, p.tok("cV")], inc="cP")
        t_bb1 = p.emit("vector", lambda v: v.tensor_copy(b2bc[:, 0:512], pb2),
                       waits=[t_bb, t_m2_prev])
        t_bb = p.emit("tensor", lambda te: te.matmul(
            pb2, ones1f[:, :], b2row[:, 512:1024], start=True, stop=True),
            waits=[t_bb1], inc="cP")
        t_bb_prev = t_bb
        t_bb2 = p.emit("vector", lambda v: v.tensor_copy(b2bc[:, 512:1024],
                                                         pb2), waits=[t_bb])
        for si in range(L0_CAP // 128):
            t_vl = p.dma("sync", xg[:, :],
                         vst[e * L0_CAP + si * 128:e * L0_CAP + (si + 1) * 128,
                             0:D],
                         sem="dVl",
                         waits=[("dScat", n_scat)] + tr_toks[-1:] +
                               ([t_vlp] if si + e else []))
            t_vlp = t_vl
            for f in range(NF):
                fb = len(tr_toks) % 2
                t_tr = p.emit("tensor", lambda te, f=f, fb=fb: te.transpose(
                    ptb[fb][:, :], xg[:, f * 128:(f + 1) * 128], identb[:, :]),
                    waits=[t_vl, t_idb] +
                          ([tr_toks[-2]] if len(tr_toks) >= 2 else []),
                    inc="cP")
                tr_toks.append(p.emit(
                    "scalar", lambda s, f=f, si=si, fb=fb: s.copy(
                        xgT[:, f * L0_CAP + si * 128:f * L0_CAP + (si + 1) * 128],
                        ptb[fb][:, :]), waits=[t_tr, t_m1_prev]))
        FFQ = FF // 4
        for ffq in range(4):
            px = (4 * e + ffq) % 2
            t_w1 = p.dma("gpsimd",
                         w1sb[px][:, :].rearrange("p (c f) -> p c f", c=NF),
                         w1[e, :, ffq * FFQ:(ffq + 1) * FFQ].rearrange(
                             "(c p) f -> p c f", p=128), sem="dW1",
                         waits=[w1_rd.get(px), w1_ld.get("t")])
            w1_ld["t"] = t_w1
            for ffl in range(NFF // 4):
                ffb = ffq * (NFF // 4) + ffl
                bk = ffb % 2

                def m1(te, ffl=ffl, bk=bk, px=px):
                    ins = None
                    for f in range(NF):
                        ins = te.matmul(hps[bk],
                                        w1sb[px][:, f * FFQ + ffl * 128:
                                                 f * FFQ + (ffl + 1) * 128],
                                        xgT[:, f * L0_CAP:(f + 1) * L0_CAP],
                                        start=(f == 0), stop=(f == NF - 1))
                    return ins
                t_m1 = p.emit("tensor", m1, waits=[t_w1] + tr_toks[-2:] +
                              [p.tok("cA")], inc="cP")
                w1_rd[px] = t_m1
                t_m1_prev = t_m1
                t_relu_prev = p.emit(
                    "scalar", lambda s, ffb=ffb, bk=bk: s.activation(
                        hT[:, ffb * L0_CAP:(ffb + 1) * L0_CAP], hps[bk],
                        AF.Relu, bias=b1sb[:, ffb:ffb + 1]),
                    waits=[t_m1, t_b1, t_m2_prev])
        t_ev = {}
        for dh in range(2):
            px = (2 * e + dh) % 2
            t_w2 = p.dma("gpsimd",
                         w2sb[px][:, :].rearrange("p (c d) -> p c d", c=NFF),
                         w2[e, :, dh * 512:(dh + 1) * 512].rearrange(
                             "(c p) d -> p c d", p=128),
                         sem="dW2", waits=[w2_rd.get(px), w2_ld.get("t")])
            w2_ld["t"] = t_w2
            for si in range(L0_CAP // 128):
                bk = dh

                def m2(te, si=si, dh=dh, bk=bk, px=px):
                    ins = None
                    for ffb in range(NFF):
                        ins = te.matmul(
                            yps[bk],
                            hT[:, ffb * L0_CAP + si * 128:
                               ffb * L0_CAP + (si + 1) * 128],
                            w2sb[px][:, ffb * 512:(ffb + 1) * 512],
                            start=(ffb == 0), stop=(ffb == NFF - 1))
                    return ins
                t_m2 = p.emit("tensor", m2,
                              waits=[t_w2, p.tok("cA"), p.tok("cV")], inc="cP")
                w2_rd[px] = t_m2
                t_m2_prev = t_m2
                t_ev[(si, dh)] = p.emit(
                    "vector", lambda v, si=si, dh=dh, bk=bk: v.tensor_tensor(
                        ysb[si][:, dh * 512:dh * 512 + 512], yps[bk],
                        b2bc[:, dh * 512:dh * 512 + 512], ALU.add),
                    waits=[t_m2, t_bb2, p.tok("dY")])
        for si in range(L0_CAP // 128):
            p.dma("sync", ybuf[e * L0_CAP + si * 128:
                               e * L0_CAP + (si + 1) * 128, :],
                  ysb[si][:, :], sem="dY",
                  waits=[t_ev[(si, 0)], t_ev[(si, 1)]])

    n_y = p.n("dY")
    NST0 = L0_NSLOT // 128
    idr = [p.sb(f"idr{i}", [128, 4], BF16) for i in range(2)]
    t_idr = {0: None, 1: None}
    t_idc = []
    t_lprev = None
    for st in range(NST0):
        bk = st % 2
        t_l = p.dma("sync", idr[bk][:, :],
                    vst[st * 128:(st + 1) * 128, D:D + 4], sem="dIr",
                    waits=[("dScat", n_scat), t_lprev] +
                          ([t_idr[bk]] if t_idr[bk] else []))
        t_lprev = t_l
        t_1 = p.emit("vector", lambda v, bk=bk: v.tensor_tensor(
            adjc[:, 0:1], idr[bk][:, 0:1], idr[bk][:, 1:2], ALU.add),
            waits=[t_l] + t_idc[-1:])
        t_2 = p.emit("vector", lambda v, bk=bk: v.tensor_scalar(
            adjc[:, 1:2], idr[bk][:, 2:3], -2.0e9, 2.0e9, ALU.mult, ALU.add),
            waits=[t_l] + t_idc[-1:])
        t_3 = p.emit("vector", lambda v, st=st: v.tensor_tensor(
            idcols[:, st:st + 1], adjc[:, 0:1], adjc[:, 1:2], ALU.add),
            waits=[t_1, t_2])
        t_idr[bk] = t_3
        t_idc.append(t_3)
    t_ti = p.emit("gpsimd", lambda g: g.iota(
        tIot[:, :], [[1, 128]], base=0, channel_multiplier=0,
        allow_small_or_imprecise_dtypes=True))
    for g0 in range(0, NT, 3):
        tts = list(range(g0, min(g0 + 3, NT)))
        p.fence(dma_sems=("dO", "dYl", "dX"))
        t_rd = {0: None, 1: None}
        t_psrd = {0: None, 1: None}
        t_lastmm = {}
        t_ylp = {0: None, 1: None}
        for st in range(NST0):
            bk = st % 2
            t_yl = p.dma("sync" if bk == 0 else "scalar",
                         yld[bk][:, :],
                         ybuf[st * 128:(st + 1) * 128, :],
                         sem="dYl" if bk == 0 else "dYl2",
                         waits=[("dY", n_y), t_ylp[bk]] +
                               ([t_rd[bk]] if t_rd[bk] else []))
            t_ylp[bk] = t_yl
            for i, tt in enumerate(tts):
                pb = (st * len(tts) + i) % 2
                t_ps = p.emit("vector", lambda v, tt=tt, st=st, pb=pb:
                              v.tensor_scalar(
                                  Psm[pb][:, :], tIot[:, :], float(tt * 128),
                                  idcols[:, st:st + 1], ALU.add, ALU.is_equal),
                              waits=[t_idc[st], t_ti] +
                                    ([t_psrd[pb]] if t_psrd[pb] else []))

                def mm(te, i=i, pb=pb, bk=bk, fi=(st == 0),
                       la=(st == NST0 - 1)):
                    te.matmul(pbk[2 * i][:, :], Psm[pb][:, :],
                              yld[bk][:, 0:512], start=fi, stop=la)
                    return te.matmul(pbk[2 * i + 1][:, :], Psm[pb][:, :],
                                     yld[bk][:, 512:1024], start=fi, stop=la)
                t_mm = p.emit("tensor", mm, waits=[t_ps, t_yl], inc="cP")
                t_psrd[pb] = t_mm
                t_rd[bk] = t_mm
                t_lastmm[i] = t_mm
        for i, tt in enumerate(tts):
            bk2 = tt % 2
            t_x = p.dma("sync", xt[bk2][:, :], x[tt * 128:(tt + 1) * 128, :],
                        sem="dX", waits=[p.tok("cV")])
            t_o0 = p.emit("vector", lambda v, i=i, tt=tt, bk2=bk2:
                          v.scalar_tensor_tensor(
                              xot[:, 0:512], pbk[2 * i][:, :],
                              ghalf[:, tt:tt + 1], xt[bk2][:, 0:512],
                              ALU.mult, ALU.add),
                          waits=[t_lastmm[i], t_x, p.tok("dO")])
            t_o1 = p.emit("vector", lambda v, i=i, tt=tt, bk2=bk2:
                          v.scalar_tensor_tensor(
                              xot[:, 512:1024], pbk[2 * i + 1][:, :],
                              ghalf[:, tt:tt + 1], xt[bk2][:, 512:1024],
                              ALU.mult, ALU.add),
                          waits=[t_lastmm[i], t_x, p.tok("dO")])
            p.dma("sync", xo[tt * 128:(tt + 1) * 128, :], xot[:, :], sem="dO",
                  waits=[t_o0, t_o1])


SLOPES = [2.0 ** (-8.0 / 16 * (h + 1)) for h in range(16)]
BIGM = 30000.0


def build_ffn0():
    nc = bass.Bass("TRN2", target_bir_lowering=False)
    p = P(nc)
    io = {}
    io["x"] = p.dram("x", [TOWN, D], F32, kind="ExternalInput")
    io["idxg"] = p.dram("idxg", [T, 1], F32, kind="ExternalInput")
    io["oidx"] = p.dram("oidx", [TOWN, 1], F32, kind="ExternalInput")
    io["ogate"] = p.dram("ogate", [TOWN, 1], F32, kind="ExternalInput")
    io["owntiles"] = p.dram("owntiles", [1, NT], F32, kind="ExternalInput")
    io["w1"] = p.dram("w1", [L0_E, D, FF], F32, kind="ExternalInput")
    io["b1"] = p.dram("b1", [L0_E, FF], F32, kind="ExternalInput")
    io["w2"] = p.dram("w2", [L0_E, FF, D], F32, kind="ExternalInput")
    io["b2"] = p.dram("b2", [L0_E, D], F32, kind="ExternalInput")
    io["xo"] = p.dram("xo", [TOWN, D], F32, kind="ExternalOutput")
    emit_ffn0(p, io)
    return p.finalize(out_sems=["dO"])


def emit_attn0(p, io):
    xfull = io["xfull"]
    xq = io["xq"]
    posq = io["posq"]
    w_inT = io["w_inT"]
    w_outT = io["w_outT"]
    xo = io["xo"]
    qxbuf = p.dram("qxbuf", [5, 512], BF16)

    HW = 8
    NQG = 4
    _dbg_pts = p.dram("_dbg_pts", [8 * 128, 512], BF16,
                      kind="ExternalOutput") if DEBUG else None
    _dbg_qex = p.dram("_dbg_qex", [5, 512], BF16,
                      kind="ExternalOutput") if DEBUG else None
    ident, t_id = make_ident(p, BF16, "identb")
    xt = [p.sb(f"xt{i}", [128, D], F32) for i in range(2)]
    sq = p.sb("sq", [128, D], F32)
    st = p.sb("st", [128, 8], F32)
    xn = p.sb("xn", [128, D], BF16)
    xnT = p.sb("xnT", [128, D], BF16)
    wsb = p.sb("wsb", [128, NF * 1024], BF16)
    kT = p.sb("kT", [128, 4 * S], BF16)
    vsb = p.sb("vsb", [128, NTB * 520], BF16)
    qT = p.sb("qT", [128, 4 * TOWN], BF16)
    ctxT = p.sb("ctxT", [128, 8 * TOWN], BF16)
    posq_r = p.sb("posq_r", [1, TOWN], F32)
    kex = p.sb("kex", [5, S], BF16)
    qex = p.sb("qex", [5, 512], BF16)
    rowbuf = p.sb("rowbuf", [1, S], BF16)
    qhl = p.sb("qhl", [1, 3 * 512], F32)
    kmax8 = p.sb("kmax8", [1, HW], F32)
    qsq = p.sb("qsq", [64, 512], BF16)
    ksq = p.sb("ksq", [64, 512], BF16)
    kmax = p.sb("kmax", [1, 4], F32)
    onec64 = p.sb("onec64", [64, 1], BF16)
    ones1f = p.sb("ones1f", [1, 128], F32)
    pqbc = p.sb("pqbc", [128, 512], F32)
    poskc = p.sb("poskc", [128, 1], F32)
    mneg = p.sb("mneg", [128, 8 * 512], BF16)
    smask = p.sb("smask", [128, 512], F32)
    ptsb = p.sb("ptsb", [128, 512], BF16)
    cxev = p.sb("cxev", [65, 512], F32)
    rrow = p.sb("rrow", [1, 512], F32)
    rbc = p.sb("rbc", [64, 512], F32)
    ppt = [p.ps(f"ppt{i}", [128, 512], F32) for i in range(2)]
    pctx = p.ps("pctx", [65, 512], F32)
    ppj = [p.ps(f"ppj{i}", [128, 512], F32) for i in range(2)]
    prow = p.ps("prow", [1, 512], F32)
    ptb = [p.ps("ptb0", [128, 128], BF16), p.ps("ptb1", [128, 128], BF16)]

    t_pq = p.dma("sync", posq_r[:, :],
                 posq.rearrange("(o t) one -> o (t one)", o=1), sem="dPq")
    p.emit("vector", lambda v: v.memset(ctxT[:, :], 0.0))
    t_k1 = p.emit("gpsimd", lambda g: g.iota(
        rowbuf[0:1, :].rearrange("o (a b) -> o a b", a=S // 256),
        [[256, S // 256], [0, 256]], base=0, channel_multiplier=0,
        allow_small_or_imprecise_dtypes=True))
    t_k1 = p.dma("sync", kex[0:1, :], rowbuf[:, :], sem="dKx", waits=[t_k1])
    t_k2 = p.emit("gpsimd", lambda g: g.iota(
        rowbuf[0:1, :].rearrange("o (a b) -> o a b", a=S // 256),
        [[0, S // 256], [1, 256]], base=0, channel_multiplier=0,
        allow_small_or_imprecise_dtypes=True), waits=[t_k1])
    t_k2 = p.dma("sync", kex[1:2, :], rowbuf[:, :], sem="dKx", waits=[t_k2])
    t_k3 = p.emit("vector", lambda v: v.memset(rowbuf[0:1, :], 1.0),
                  waits=[t_k2])
    for _r in range(2, 5):
        t_k3 = p.dma("sync", kex[_r:_r + 1, :], rowbuf[:, :], sem="dKx",
                     waits=[t_k3])
    t_kex = p.tok("dKx")
    t_o64 = p.emit("vector", lambda v: v.memset(onec64[:, :], 1.0))
    t_1f = p.emit("vector", lambda v: v.memset(ones1f[:, :], 1.0))


    def ln_and_T(src, tile, bk):
        t_x = p.dma("sync", xt[bk][:, :], src[tile * 128:(tile + 1) * 128, :],
                    sem="dX", waits=[p.tok("cV"), p.tok("cA")])
        t_ln = ln_tile(p, xn, xt[bk], st, sq, waits=[t_x])
        toks = transpose_pass(p, xnT, xn, ident, ptb, [t_ln], NF, t_id,
                              evac="scalar")
        return toks[-1]

    for hh in range(2):
        kcol0 = 1024 + hh * 512
        vcol0 = 2048 + hh * 512
        t_wk = p.dma("gpsimd",
                     wsb[:, 0:NF * 512].rearrange("p (c e) -> p c e", c=NF),
                     w_inT[:, kcol0:kcol0 + 512].rearrange(
                         "(c p) e -> p c e", p=128), sem="dWk",
                     waits=[p.tok("cP")])
        p.dma("gpsimd", wsb[:, NF * 512:].rearrange("p (c e) -> p c e", c=NF),
              w_inT[:, vcol0:vcol0 + 512].rearrange(
                  "(c p) e -> p c e", p=128), sem="dWk", waits=[p.tok("cP")])
        t_w = p.tok("dWk")
        for tb in range(NTB):
            p.fence()
            t_T = ln_and_T(xfull, tb, tb % 2)
            for cg in range(4):
                bk = cg % 2

                def mk(te, cg=cg, bk=bk):
                    ins = None
                    for f in range(NF):
                        ins = te.matmul(
                            ppj[bk][:, 0:128],
                            wsb[:, f * 512 + cg * 128:f * 512 + (cg + 1) * 128],
                            xnT[:, f * 128:(f + 1) * 128],
                            start=(f == 0), stop=(f == NF - 1))
                    return ins
                t_mk = p.emit("tensor", mk, waits=[t_w, t_T, p.tok("cV")],
                              inc="cP")
                p.emit("vector", lambda v, cg=cg, tb=tb, bk=bk: v.tensor_copy(
                    kT[:, cg * S + tb * 128:cg * S + (tb + 1) * 128],
                    ppj[bk][:, 0:128]), waits=[t_mk])

            def mv(te):
                ins = None
                for f in range(NF):
                    ins = te.matmul(ppj[0][:, :],
                                    xnT[:, f * 128:(f + 1) * 128],
                                    wsb[:, NF * 512 + f * 512:
                                        NF * 512 + (f + 1) * 512],
                                    start=(f == 0), stop=(f == NF - 1))
                return ins
            t_mv = p.emit("tensor", mv, waits=[t_w, t_T, p.tok("cA"),
                                               p.tok("cV")], inc="cP")
            vt0 = tb * 520
            for hl in range(HW):
                p.emit("scalar", lambda sc, hl=hl, vt0=vt0: sc.copy(
                    vsb[:, vt0 + hl * 65:vt0 + hl * 65 + 64],
                    ppj[0][:, hl * 64:(hl + 1) * 64]), waits=[t_mv])
            p.emit("vector", lambda v, vt0=vt0: v.memset(
                bass.AP(vsb, vt0 + 64, [[vsb.shape[1], 128], [65, HW]]), 1.0),
                waits=[t_mv])
        qcol0 = hh * 512
        p.fence()
        t_wq0 = p.dma("gpsimd",
                      wsb[:, 0:NF * 512].rearrange("p (c e) -> p c e", c=NF),
                      w_inT[:, qcol0:qcol0 + 512].rearrange(
                          "(c p) e -> p c e", p=128), sem="dWk",
                      waits=[p.tok("cP")])
        t_wq = p.tok("dWk")
        for tq in range(NT):
            p.fence()
            t_T = ln_and_T(xq, tq, tq % 2)
            for cg in range(4):
                bk = cg % 2

                def mq(te, cg=cg, bk=bk):
                    ins = None
                    for f in range(NF):
                        ins = te.matmul(
                            ppj[bk][:, 0:128],
                            wsb[:, f * 512 + cg * 128:f * 512 + (cg + 1) * 128],
                            xnT[:, f * 128:(f + 1) * 128],
                            start=(f == 0), stop=(f == NF - 1))
                    return ins
                t_mq = p.emit("tensor", mq, waits=[t_wq, t_T, p.tok("cV")],
                              inc="cP")
                p.emit("vector", lambda v, cg=cg, tq=tq, bk=bk: v.tensor_scalar(
                    qT[:, cg * TOWN + tq * 128:cg * TOWN + (tq + 1) * 128],
                    ppj[bk][:, 0:128], 0.125, None, ALU.mult), waits=[t_mq])
        p.fence()
        for hl in range(HW):
            cg, r0 = hl // 2, (hl % 2) * 64
            t_kz = p.emit("vector", lambda v: v.memset(kmax[:, 0:1], 0.0),
                          waits=[p.tok("cA"), p.tok("cV")])
            for c5 in range(S // 512):
                t_ks = p.emit("vector", lambda v, cg=cg, r0=r0, c5=c5:
                              v.tensor_tensor(
                                  ksq[:, :],
                                  kT[r0:r0 + 64, cg * S + c5 * 512:
                                     cg * S + (c5 + 1) * 512],
                                  kT[r0:r0 + 64, cg * S + c5 * 512:
                                     cg * S + (c5 + 1) * 512], ALU.mult),
                              waits=[p.tok("cP"), p.tok("cV")])
                t_mb = p.emit("tensor", lambda te: te.matmul(
                    prow[:, :], onec64[:, :], ksq[:, :],
                    start=True, stop=True), waits=[t_ks, t_o64], inc="cP")
                t_kr = p.emit("vector", lambda v: v.reduce_max(
                    kmax[0:1, 2:3], prow[:, :], axis=AX.X),
                    waits=[t_mb, t_kz])
                t_kz = p.emit("vector", lambda v: v.tensor_tensor(
                    kmax[0:1, 0:1], kmax[0:1, 0:1],
                    kmax[0:1, 2:3], ALU.max), waits=[t_kr])
            t_bs = p.emit("scalar", lambda s, hl=hl: s.activation(
                kmax8[0:1, hl:hl + 1], kmax[0:1, 0:1], AF.Sqrt), waits=[t_kz])
        for qg in range(NQG):
            p.fence()
            q0 = qg * 512
            t_qb = p.emit("tensor", lambda te, q0=q0: te.matmul(
                ppt[0][:, :], ones1f[:, :], posq_r[:, q0:q0 + 512],
                start=True, stop=True), waits=[t_pq, t_1f], inc="cP")
            t_qbc = p.emit("vector", lambda v: v.tensor_copy(pqbc[:, :],
                                                             ppt[0][:, :]),
                           waits=[t_qb])
            t_mn = []
            for mi in range(8):
                kt = 8 * qg + mi
                t_pk = p.emit("gpsimd", lambda g, kt=kt: g.iota(
                    poskc[:, :], [[1, 1]], base=kt * 128, channel_multiplier=1,
                    allow_small_or_imprecise_dtypes=True),
                    waits=[p.tok("cG"), p.tok("cV")])
                t_mn.append(p.emit("vector", lambda v, mi=mi: v.tensor_scalar(
                    mneg[:, mi * 512:(mi + 1) * 512], pqbc[:, :],
                    poskc[:, 0:1], -BIGM, ALU.is_lt, ALU.mult),
                    waits=[t_pk, t_qbc]))
            kext = 8 * qg + 8
            t_ce = None
            for hl in range(HW):
                h_abs = hh * 8 + hl
                sl = SLOPES[h_abs]
                cg, r0 = hl // 2, (hl % 2) * 64
                t_q1 = p.emit("vector", lambda v, sl=sl, q0=q0: v.tensor_scalar(
                    qhl[0:1, 0:512], posq_r[:, q0:q0 + 512], -sl, None,
                    ALU.mult), waits=[p.tok("cP"), p.tok("dQx")])
                t_q2 = p.emit("vector", lambda v: v.tensor_copy(
                    rowbuf[0:1, 1024:1536], qhl[0:1, 0:512]),
                    waits=[t_q1, t_kex, p.tok("dQx")])
                t_q3 = p.emit("vector", lambda v: v.tensor_copy(
                    qhl[0:1, 512:1024], rowbuf[0:1, 1024:1536]), waits=[t_q2])
                t_q4 = p.emit("vector", lambda v: v.tensor_tensor(
                    qhl[0:1, 1024:1536], qhl[0:1, 0:512], qhl[0:1, 512:1024],
                    ALU.subtract), waits=[t_q3])
                t_q5 = p.emit("vector", lambda v: v.tensor_copy(
                    rowbuf[0:1, 1536:2048], qhl[0:1, 1024:1536]), waits=[t_q4])
                t_q6 = p.emit("vector", lambda v, sl=sl: v.memset(
                    rowbuf[0:1, 0:1024], sl), waits=[p.tok("dQx"), t_kex])
                t_b0 = p.emit("vector", lambda v, cg=cg, r0=r0, q0=q0:
                              v.tensor_tensor(
                                  qsq[:, :],
                                  qT[r0:r0 + 64, cg * TOWN + q0:
                                     cg * TOWN + q0 + 512],
                                  qT[r0:r0 + 64, cg * TOWN + q0:
                                     cg * TOWN + q0 + 512], ALU.mult),
                              waits=[t_q6, p.tok("cP")])
                t_b1 = p.emit("tensor", lambda te: te.matmul(
                    prow[:, :], onec64[:, :], qsq[:, :], start=True, stop=True),
                    waits=[t_b0, t_o64], inc="cP")
                t_b2 = p.emit("scalar", lambda s: s.activation(
                    qhl[0:1, 1024:1536], prow[:, :], AF.Sqrt), waits=[t_b1])
                t_q7 = p.emit("vector", lambda v, hl=hl: v.tensor_scalar(
                    rowbuf[0:1, 2048:2560], qhl[0:1, 1024:1536],
                    kmax8[0:1, hl:hl + 1], -1.0, ALU.mult, ALU.mult),
                    waits=[t_b2, t_bs])
                t_q4a = p.dma("sync", qxbuf.rearrange("r c -> () (r c)"),
                              rowbuf[0:1, 0:2560],
                              sem="dQx", waits=[t_q5, t_q7, p.tok("cP")])
                t_q4 = p.dma("sync", qex[:, :], qxbuf[:, :],
                             sem="dQx", waits=[t_q4a])
                if DEBUG and hh == 0 and qg == 0 and hl == 0:
                    p.dma("sync", _dbg_qex[:, :], qex[:, :], sem="dDbg2",
                          waits=[t_q4])
                for kt in range(kext):
                    masked = kt >= 8 * qg
                    bk = kt % 2

                    def ms(te, cg=cg, r0=r0, kt=kt, bk=bk, q0=q0):
                        te.matmul(ppt[bk][:, :],
                                  kT[r0:r0 + 64,
                                     cg * S + kt * 128:cg * S + (kt + 1) * 128],
                                  qT[r0:r0 + 64, cg * TOWN + q0:
                                     cg * TOWN + q0 + 512],
                                  start=True, stop=False)
                        return te.matmul(
                            ppt[bk][:, :], kex[:, kt * 128:(kt + 1) * 128],
                            qex[:, :], start=False, stop=True)
                    t_s = p.emit("tensor", ms, waits=[t_q4, p.tok("cV"),
                                                      p.tok("cA")], inc="cP")
                    if masked:
                        mi = kt - 8 * qg
                        t_cl = p.emit("vector", lambda v, bk=bk, mi=mi:
                                      v.tensor_tensor(
                                          smask[:, :], ppt[bk][:, :],
                                          mneg[:, mi * 512:(mi + 1) * 512],
                                          ALU.add), waits=[t_s, t_mn[mi]])
                        t_e = p.emit("scalar", lambda s: s.activation(
                            ptsb[:, :], smask[:, :], AF.Exp),
                            waits=[t_cl, p.tok("cP"), p.tok("dDbg2")])
                    else:
                        t_e = p.emit("scalar", lambda s, bk=bk: s.activation(
                            ptsb[:, :], ppt[bk][:, :], AF.Exp),
                            waits=[t_s, p.tok("cP"), p.tok("dDbg2")])

                    def mc(te, kt=kt, hl=hl, first=(kt == 0),
                           last=(kt == kext - 1)):
                        return te.matmul(
                            pctx[:, :],
                            vsb[:, kt * 520 + hl * 65:kt * 520 + (hl + 1) * 65],
                            ptsb[:, :], start=first, stop=last)
                    t_c = p.emit("tensor", mc,
                                 waits=[t_e] + ([t_ce] if kt == 0 and t_ce
                                                else []), inc="cP")
                    if DEBUG and hh == 0 and qg == 0 and hl == 0:
                        p.dma("sync", _dbg_pts[kt * 128:(kt + 1) * 128, :],
                              ptsb[:, :], sem="dDbg2", waits=[t_e, t_c])
                t_ce = p.emit("vector", lambda v: v.tensor_copy(cxev[:, :],
                                                                pctx[:, :]),
                              waits=[t_c])
                t_r1 = p.emit("scalar", lambda s: s.activation(
                    rrow[:, :], cxev[64:65, :], AF.Ln), waits=[t_ce])
                t_r2 = p.emit("scalar", lambda s: s.activation(
                    rrow[:, :], rrow[:, :], AF.Exp, scale=-1.0), waits=[t_r1])
                t_rb = p.emit("tensor", lambda te: te.matmul(
                    ppt[0][0:64, :], ones1f[:, 0:64], rrow[:, :],
                    start=True, stop=True), waits=[t_r2, p.tok("cV")], inc="cP")
                t_rbc = p.emit("vector", lambda v: v.tensor_copy(
                    rbc[:, :], ppt[0][0:64, :]), waits=[t_rb])
                p.emit("vector", lambda v, hh=hh, cg=cg, r0=r0, q0=q0:
                       v.tensor_tensor(
                           ctxT[r0:r0 + 64, (4 * hh + cg) * TOWN + q0:
                                (4 * hh + cg) * TOWN + q0 + 512],
                           cxev[0:64, :], rbc[:, :], ALU.mult),
                       waits=[t_rbc])
    if DEBUG:
        d_kT = p.dram("d_kT", [128, 4 * S], BF16, kind="ExternalOutput")
        d_qT = p.dram("d_qT", [128, 4 * TOWN], BF16, kind="ExternalOutput")
        d_vsb = p.dram("d_vsb", [128, NTB * 520], BF16, kind="ExternalOutput")
        d_ctxT = p.dram("d_ctxT", [128, 8 * TOWN], BF16, kind="ExternalOutput")
        d_kex = p.dram("d_kex", [5, S], BF16, kind="ExternalOutput")
        d_km8 = p.dram("d_km8", [1, HW], F32, kind="ExternalOutput")
        wd = [p.tok("cV"), p.tok("cA"), p.tok("cP"), p.tok("cG")]
        p.dma("sync", d_kT[:, :], kT[:, :], sem="dDbg", waits=wd)
        p.dma("sync", d_qT[:, :], qT[:, :], sem="dDbg", waits=wd)
        p.dma("sync", d_vsb[:, :], vsb[:, :], sem="dDbg", waits=wd)
        p.dma("sync", d_ctxT[:, :], ctxT[:, :], sem="dDbg", waits=wd)
        p.dma("sync", d_kex[:, :], kex[:, :], sem="dDbg", waits=wd)
        p.dma("sync", d_km8[:, :], kmax8[:, :], sem="dDbg", waits=wd)
        p.emit("gpsimd", lambda g: g.wait_ge(p._sems["dDbg"], p.cnt["dDbg"]),
               inc=None)

    p.fence()
    t_wo = p.dma("gpsimd", wsb[:, :].rearrange("p (c d) -> p c d", c=NF),
                 w_outT.rearrange("(c p) d -> p c d", p=128), sem="dWo",
                 waits=[p.tok("cP")])
    for tq in range(NT):
        bk = tq % 2
        p.fence()
        t_x = p.dma("sync", xt[bk][:, :], xq[tq * 128:(tq + 1) * 128, :],
                    sem="dX", waits=[p.tok("cV"), p.tok("dO")])
        for dh in range(2):
            def mo(te, tq=tq, dh=dh):
                ins = None
                for cg8 in range(8):
                    ins = te.matmul(
                        ppj[dh][:, :],
                        ctxT[:, cg8 * TOWN + tq * 128:
                             cg8 * TOWN + (tq + 1) * 128],
                        wsb[:, cg8 * D + dh * 512:cg8 * D + dh * 512 + 512],
                        start=(cg8 == 0), stop=(cg8 == 7))
                return ins
            t_mo = p.emit("tensor", mo, waits=[p.tok("cV"), p.tok("dWo"),
                                               p.tok("cA")], inc="cP")
            p.emit("vector", lambda v, dh=dh, bk=bk: v.tensor_tensor(
                xt[bk][:, dh * 512:dh * 512 + 512], ppj[dh][:, :],
                xt[bk][:, dh * 512:dh * 512 + 512], ALU.add),
                waits=[t_mo, t_x])
        p.dma("sync", xo[tq * 128:(tq + 1) * 128, :], xt[bk][:, :], sem="dO",
              waits=[p.tok("cV")])


def build_attn0():
    nc = bass.Bass("TRN2", target_bir_lowering=False)
    p = P(nc)
    io = {}
    io["xfull"] = p.dram("xfull", [S, D], F32, kind="ExternalInput")
    io["xq"] = p.dram("xq", [TOWN, D], F32, kind="ExternalInput")
    io["posq"] = p.dram("posq", [TOWN, 1], F32, kind="ExternalInput")
    io["w_inT"] = p.dram("w_inT", [D, 3 * D], F32, kind="ExternalInput")
    io["w_outT"] = p.dram("w_outT", [D, D], F32, kind="ExternalInput")
    io["xo"] = p.dram("xo", [TOWN, D], F32, kind="ExternalOutput")
    emit_attn0(p, io)
    return p.finalize(out_sems=["dO"])


def emit_ffn1(p, io):
    x = io["x"]
    routers = io["routers"]
    w1 = io["w1"]
    w2 = io["w2"]
    xo = io["xo"]
    ybuf = p.dram("ybuf", [L1_NSLOT, D], BF16)

    NST = L1_NSLOT // 128
    identf, t_idf = make_ident(p, F32, "identf")
    identb, t_idb = make_ident(p, BF16, "identb")
    xt = [p.sb(f"xt{i}", [128, D], F32) for i in range(2)]
    sq = p.sb("sq", [128, D], F32)
    st = p.sb("st", [128, 8], F32)
    vtiles = p.sb("vtiles", [128, NT * 1024], BF16)
    idtiles = p.sb("idtiles", [128, NT * 4], BF16)
    xnf = p.sb("xnf", [128, D], F32)
    xnT = p.sb("xnT", [128, D], F32)
    rwsb = p.sb("rwsb", [128, NF * L1_E], F32)
    lg = p.sb("lg", [128, L1_E], F32)
    mx = p.sb("mx", [128, 8], F32)
    mi = p.sb("mi", [128, 8], U32)
    triUf = p.sb("triUf", [128, 128], F32)
    ones1f = p.sb("ones1f", [1, 128], F32)
    onec128 = p.sb("onec128", [128, 1], F32)
    eio = p.sb("eio", [128, L1_E], F32)
    emask = p.sb("emask", [128, L1_E], F32)
    lrank = p.sb("lrank", [128, L1_E], F32)
    lcarry = p.sb("lcarry", [1, L1_E], F32)
    cols = p.sb("cols", [128, 8], F32)
    icol = p.sb("icol", [128, 1], F32)
    destf = p.sb("destf", [128, NT], F32)
    gaincol = p.sb("gaincol", [128, NT], F32)
    sIot = p.sb("sIot", [128, L1_CAP], F32)
    tIot = p.sb("tIot", [128, 128], F32)
    PTall = p.sb("PTall", [128, NT * L1_CAP], BF16)
    dtmp = p.sb("dtmp", [128, 1], F32)
    w1sb = [p.sb(f"w1sb{i}", [128, NF * (FF // 4)], BF16) for i in range(2)]
    w2sb = [p.sb(f"w2sb{i}", [128, NFF * (D // 2)], BF16) for i in range(2)]
    xgT = p.sb("xgT", [128, NF * L1_CAP], BF16)
    idrows = p.sb("idrows", [4, L1_CAP], BF16)
    idcols = p.sb("idcols", [128, NST], F32)
    hT = p.sb("hT", [128, NFF * L1_CAP], BF16)
    ysb = p.sb("ysb", [128, D], BF16)
    yst = [p.sb(f"yst{i}", [128, D], BF16) for i in range(2)]
    Psm = [p.sb(f"Psm{i}", [128, 128], BF16) for i in range(2)]
    ytok = p.sb("ytok", [128, D], F32)
    pbk = [p.ps(f"pbk{i}", [128, 512], F32) for i in range(7)]
    ptb = p.ps("ptb", [128, 128], BF16)
    ptf = [pbk[0][:, 0:128], pbk[1][:, 0:128]]
    plg = pbk[2][:, 0:L1_E]
    plincl = pbk[2][:, 128:128 + L1_E]
    pltot = pbk[3][0:1, 256:256 + L1_E]
    hps = [pbk[4][:, 0:L1_CAP], pbk[5][:, 0:L1_CAP]]
    psort = pbk[6][:, 0:L1_CAP]
    pidr = pbk[6][0:4, 0:L1_CAP]

    t_tri = p.emit("vector", lambda v: v.memset(triUf[:, :], 1.0))
    t_tri = p.emit("gpsimd", lambda g: g.affine_select(
        triUf[:, :], triUf[:, :], [[1, 128]], ALU.is_ge, 0.0,
        channel_multiplier=-1), waits=[t_tri])
    t_1f = p.emit("vector", lambda v: v.memset(ones1f[:, :], 1.0))
    t_1c = p.emit("vector", lambda v: v.memset(onec128[:, :], 1.0))
    t_ei = p.emit("gpsimd", lambda g: g.iota(eio[:, :], [[1, L1_E]], base=0,
                                             channel_multiplier=0,
                                             allow_small_or_imprecise_dtypes=True))
    t_si = p.emit("gpsimd", lambda g: g.iota(sIot[:, :], [[1, L1_CAP]], base=0,
                                             channel_multiplier=0,
                                             allow_small_or_imprecise_dtypes=True))
    t_ti = p.emit("gpsimd", lambda g: g.iota(tIot[:, :], [[1, 128]], base=0,
                                             channel_multiplier=0,
                                             allow_small_or_imprecise_dtypes=True))
    t_idc = []
    t_plo = p.emit("gpsimd", lambda g: g.iota(
        cols[:, 0:1], [[1, 1]], base=0, channel_multiplier=1,
        allow_small_or_imprecise_dtypes=True), waits=[p.tok("cV")])
    for t in range(NT):
        t_i4 = p.emit("vector", lambda v, t=t: v.memset(
            idtiles[:, t * 4 + 0:t * 4 + 1], float(t * 128)))
        t_i5 = p.emit("vector", lambda v, t=t: v.tensor_copy(
            idtiles[:, t * 4 + 1:t * 4 + 2], cols[:, 0:1]),
            waits=[t_plo, t_i4])
        t_i6 = p.emit("vector", lambda v, t=t: v.memset(
            idtiles[:, t * 4 + 2:t * 4 + 4], 1.0), waits=[t_i5])
        t_idc.append(t_i6)

    w1_rd, w1_ld, w2_rd, w2_ld = {}, {}, {}, {}
    for rnd in range(4):
        t_lc = p.emit("vector", lambda v: v.memset(lcarry[:, :], 0.0),
                      waits=[p.tok("cP")])
        t_rw = p.dma("gpsimd", rwsb[:, :].rearrange("p (c e) -> p c e", c=NF),
                     routers[rnd, :, :].rearrange("(c p) e -> p c e", p=128),
                     sem="dRw", waits=[p.tok("cP")])
        t_l_prev = None
        for t in range(NT):
            bk = t % 2
            if rnd == 0:
                t_x = p.dma("sync", xt[bk][:, :], x[t * 128:(t + 1) * 128, :],
                            sem="dX", waits=[p.tok("cV"), p.tok("cA")])
                t_v = ln_tile(p, xnf, xt[bk], st, sq, waits=[t_x])
                t_v = p.emit("vector", lambda v, t=t: v.tensor_copy(
                    vtiles[:, t * 1024:(t + 1) * 1024], xnf[:, :]),
                    waits=[t_v])
            else:
                t_v = p.emit("vector", lambda v, t=t: v.tensor_copy(
                    xnf[:, :], vtiles[:, t * 1024:(t + 1) * 1024]),
                    waits=[p.tok("cA"), p.tok("cP")])
            t_tr = transpose_pass(p, xnT, xnf, identf, ptf,
                                  [t_v, p.tok("cV")], NF, t_idf)

            def mm(te):
                ins = None
                for f in range(NF):
                    ins = te.matmul(plg, xnT[:, f * 128:(f + 1) * 128],
                                    rwsb[:, f * L1_E:(f + 1) * L1_E],
                                    start=(f == 0), stop=(f == NF - 1))
                return ins
            t_mm = p.emit("tensor", mm, waits=[t_rw, t_l_prev] + t_tr,
                          inc="cP")
            t_l = p.emit("vector", lambda v: v.tensor_copy(lg[:, :], plg),
                         waits=[t_mm])
            t_l_prev = t_l
            t_m0 = p.emit("vector", lambda v: v.memset(mx[:, :], -1e30))
            t_m1 = p.emit("vector", lambda v: v.reduce_max(
                mx[:, 0:1], lg[:, :], axis=AX.X), waits=[t_l, t_m0])
            t_ix = p.emit("vector", lambda v: v.max_index(
                mi[:, :], mx[:, :], lg[:, :]), waits=[t_m1])
            t_nm = p.emit("vector", lambda v: v.tensor_scalar(
                mx[:, 1:2], mx[:, 0:1], -1.0, None, ALU.mult),
                waits=[t_m1, t_ix])
            t_e = p.emit("scalar", lambda s: s.activation(
                lg[:, :], lg[:, :], AF.Exp, bias=mx[:, 1:2]),
                waits=[t_nm, t_ix])
            t_s = p.emit("vector", lambda v: v.reduce_sum(
                mx[:, 2:3], lg[:, :], axis=AX.X), waits=[t_e])
            p.emit("vector", lambda v, t=t: v.reciprocal(
                gaincol[:, t:t + 1], mx[:, 2:3]), waits=[t_s])
            t_i2 = p.emit("vector", lambda v: v.tensor_copy(
                icol[:, :], mi[:, 0:1]), waits=[t_ix])
            t_em2 = p.emit("vector", lambda v: v.tensor_scalar(
                emask[:, :], eio[:, :], icol[:, :], None, ALU.is_equal),
                waits=[t_i2, t_ei])
            t_li = p.emit("tensor", lambda te: te.matmul(
                plincl, triUf[:, :], emask[:, :], start=True, stop=False),
                waits=[t_em2, t_tri, t_lc, t_1f], inc="cP")
            t_li = p.emit("tensor", lambda te: te.matmul(
                plincl, ones1f[:, :], lcarry[:, :], start=False, stop=True),
                inc="cP", waits=[t_li])
            t_lr = p.emit("vector", lambda v: v.tensor_copy(
                lrank[:, :], plincl), waits=[t_li])
            t_lt = p.emit("tensor", lambda te: te.matmul(
                pltot, onec128[:, :], emask[:, :], start=True, stop=True),
                waits=[t_em2, t_1c, t_lr], inc="cP")
            t_lc = p.emit("vector", lambda v: v.tensor_tensor(
                lcarry[:, :], lcarry[:, :], pltot, ALU.add),
                waits=[t_lt, t_lc])
            t_krm = p.emit("vector", lambda v: v.tensor_tensor(
                lrank[:, :], lrank[:, :], emask[:, :], ALU.mult), waits=[t_lr])
            t_lrt = p.emit("vector", lambda v: v.reduce_sum(
                cols[:, 4:5], lrank[:, :], axis=AX.X), waits=[t_krm])
            t_d0 = p.emit("vector", lambda v: v.tensor_scalar(
                cols[:, 5:6], icol[:, :], float(L1_CAP), -1.0, ALU.mult,
                ALU.add), waits=[t_i2])
            t_d1 = p.emit("vector", lambda v: v.tensor_tensor(
                cols[:, 5:6], cols[:, 5:6], cols[:, 4:5], ALU.add),
                waits=[t_d0, t_lrt])
            t_ov = p.emit("vector", lambda v: v.tensor_scalar(
                cols[:, 6:7], cols[:, 4:5], float(L1_CAP), None, ALU.is_gt),
                waits=[t_lrt])
            t_d2 = p.emit("vector", lambda v: v.scalar_tensor_tensor(
                cols[:, 5:6], cols[:, 6:7], 2.0e9, cols[:, 5:6], ALU.mult,
                ALU.add), waits=[t_ov, t_d1])
            p.emit("vector", lambda v, t=t: v.tensor_copy(
                destf[:, t:t + 1], cols[:, 5:6]), waits=[t_d2])
        t_mi_prev = None
        t_tp_prev = None
        t_m1_prev = None
        t_m2_prev = None
        for e in range(L1_E):
            pt_toks = []
            for ti_ in range(NT):
                t_dm = p.emit("vector", lambda v, ti_=ti_, e=e: v.tensor_scalar(
                    dtmp[:, :], destf[:, ti_:ti_ + 1], float(-e * L1_CAP),
                    None, ALU.add),
                    waits=[t_mi_prev] + (pt_toks[-1:] if pt_toks else []))
                pt_toks.append(p.emit("vector", lambda v, ti_=ti_:
                                      v.tensor_scalar(
                    PTall[:, ti_ * L1_CAP:(ti_ + 1) * L1_CAP],
                    sIot[:, :], dtmp[:, 0:1], None, ALU.is_equal),
                    waits=[t_dm, t_si]))
            for f in range(NF):
                def msort(te, f=f):
                    ins = None
                    for ti_ in range(NT):
                        ins = te.matmul(
                            psort,
                            vtiles[:, ti_ * 1024 + f * 128:
                                   ti_ * 1024 + (f + 1) * 128],
                            PTall[:, ti_ * L1_CAP:(ti_ + 1) * L1_CAP],
                            start=(ti_ == 0), stop=(ti_ == NT - 1))
                    return ins
                t_ms = p.emit("tensor", msort,
                              waits=pt_toks + [p.tok("cA"), p.tok("cV")],
                              inc="cP")
                p.emit("scalar", lambda s, f=f: s.copy(
                    xgT[:, f * L1_CAP:(f + 1) * L1_CAP], psort),
                    waits=[t_ms, t_m1_prev])

            def mid(te):
                ins = None
                for ti_ in range(NT):
                    ins = te.matmul(
                        pidr, idtiles[:, ti_ * 4:(ti_ + 1) * 4],
                        PTall[:, ti_ * L1_CAP:(ti_ + 1) * L1_CAP],
                        start=(ti_ == 0), stop=(ti_ == NT - 1))
                return ins
            t_mi = p.emit("tensor", mid, waits=pt_toks + t_idc +
                          [p.tok("cA")], inc="cP")
            t_ir = p.emit("scalar", lambda s: s.copy(idrows[:, :], pidr),
                          waits=[t_mi, t_tp_prev])
            t_mi_prev = t_mi
            for si, (c0, rows) in enumerate([(0, 128), (128, 128), (256, 64)]):
                t_tp = p.emit("tensor", lambda te, c0=c0, rows=rows:
                              te.transpose(ptb[0:rows, 0:4],
                                           idrows[:, c0:c0 + rows],
                                           identb[0:4, 0:4]),
                              waits=[t_ir, t_idb, p.tok("cV")], inc="cP")
                t_tp_prev = t_tp
                gst = (e * L1_CAP + c0) // 128
                t_cp = p.emit("vector", lambda v, rows=rows: v.tensor_copy(
                    cols[0:rows, 0:4], ptb[0:rows, 0:4]), waits=[t_tp])
                t_c1 = p.emit("vector", lambda v, rows=rows: v.tensor_tensor(
                    cols[0:rows, 4:5], cols[0:rows, 0:1], cols[0:rows, 1:2],
                    ALU.add), waits=[t_cp])
                t_c2 = p.emit("vector", lambda v, rows=rows: v.tensor_scalar(
                    cols[0:rows, 5:6], cols[0:rows, 2:3], -2.0e9, 2.0e9,
                    ALU.mult, ALU.add), waits=[t_c1])
                t_c3 = p.emit("vector", lambda v, rows=rows, gst=gst:
                              v.tensor_tensor(
                    idcols[0:rows, gst:gst + 1], cols[0:rows, 4:5],
                    cols[0:rows, 5:6], ALU.add), waits=[t_c2])
                if rows < 128:
                    p.emit("vector", lambda v, rows=rows, gst=gst: v.memset(
                        bass.AP(idcols, gst + 64 * NST * 0, [[NST, 128],
                                [1, 1]]) if False else
                        idcols[64:128, gst:gst + 1], 3.0e9),
                        waits=[t_c3, p.tok("cP")])
            FFQ = FF // 4
            for ffq in range(4):
                px = (4 * e + ffq) % 2
                t_w1 = p.dma("gpsimd",
                             w1sb[px][:, :].rearrange("p (c f) -> p c f", c=NF),
                             w1[e, :, ffq * FFQ:(ffq + 1) * FFQ].rearrange(
                                 "(c p) f -> p c f", p=128), sem="dW1",
                             waits=[w1_rd.get(px), w1_ld.get("t")])
                w1_ld["t"] = t_w1
                for ffl in range(NFF // 4):
                    ffb = ffq * (NFF // 4) + ffl
                    bk = ffb % 2

                    def m1(te, ffl=ffl, bk=bk, px=px):
                        ins = None
                        for f in range(NF):
                            ins = te.matmul(
                                hps[bk],
                                w1sb[px][:, f * FFQ + ffl * 128:
                                         f * FFQ + (ffl + 1) * 128],
                                xgT[:, f * L1_CAP:(f + 1) * L1_CAP],
                                start=(f == 0), stop=(f == NF - 1))
                        return ins
                    t_m1 = p.emit("tensor", m1, waits=[t_w1, p.tok("cA")],
                                  inc="cP")
                    w1_rd[px] = t_m1
                    t_m1_prev = t_m1
                    p.emit("scalar", lambda s, ffb=ffb, bk=bk: s.activation(
                        hT[:, ffb * L1_CAP:(ffb + 1) * L1_CAP], hps[bk],
                        AF.Silu), waits=[t_m1, t_m2_prev])
            for dh in range(2):
                px = (2 * e + dh) % 2
                t_w2 = p.dma("gpsimd",
                             w2sb[px][:, :].rearrange("p (c d) -> p c d",
                                                      c=NFF),
                             w2[e, :, dh * 512:(dh + 1) * 512].rearrange(
                                 "(c p) d -> p c d", p=128), sem="dW2",
                             waits=[w2_rd.get(px), w2_ld.get("t")])
                w2_ld["t"] = t_w2
                nsl = 0
                for si, rows in enumerate([128, 128, 64]):
                    def m2(te, nsl=nsl, rows=rows, dh=dh, px=px):
                        ins = None
                        for ffb in range(NFF):
                            ins = te.matmul(
                                pbk[dh][0:rows, :],
                                hT[:, ffb * L1_CAP + nsl:
                                   ffb * L1_CAP + nsl + rows],
                                w2sb[px][:, ffb * 512:(ffb + 1) * 512],
                                start=(ffb == 0), stop=(ffb == NFF - 1))
                        return ins
                    t_m2 = p.emit("tensor", m2,
                                  waits=[t_w2, p.tok("cA"), p.tok("cV")],
                                  inc="cP")
                    w2_rd[px] = t_m2
                    t_m2_prev = t_m2
                    p.emit("vector", lambda v, nsl=nsl, rows=rows, dh=dh:
                           v.tensor_copy(
                               ysb[0:rows, dh * 512:dh * 512 + 512],
                               pbk[dh][0:rows, :]),
                           waits=[t_m2, p.tok("dY")])
                    nsl += rows
            nsl = 0
            for si, rows in enumerate([128, 128, 64]):
                p.dma("sync", ybuf[e * L1_CAP + nsl:e * L1_CAP + nsl + rows, :],
                      ysb[0:rows, :], sem="dY", waits=[p.tok("cV")])
                nsl += rows
        n_y = p.n("dY")
        _groups = [list(range(a, min(a + 3, NT))) for a in range(0, NT, 3)]
        t_yrd = {0: None, 1: None}
        t_psrd = {0: None, 1: None}
        t_ylp = {0: None, 1: None}
        for grp in _groups:
            p.fence(dma_sems=("dO",))
            t_acc = {}
            for sti in range(NST):
                bkl = sti % 2
                t_yl = p.dma("sync" if bkl == 0 else "scalar",
                             yst[bkl][:, :],
                             ybuf[sti * 128:(sti + 1) * 128, :],
                             sem="dYl" if bkl == 0 else "dYl2",
                             waits=[("dY", n_y), t_ylp[bkl], t_yrd[bkl]])
                t_ylp[bkl] = t_yl
                for gi, tt in enumerate(grp):
                    pp = (sti * len(grp) + gi) % 2
                    t_pm = p.emit("vector", lambda v, sti=sti, tt=tt, pp=pp:
                                  v.tensor_scalar(
                        Psm[pp][:, :], tIot[:, :], float(tt * 128),
                        idcols[:, sti:sti + 1], ALU.add, ALU.is_equal),
                        waits=[t_ti, t_psrd[pp]])
                    for dh in range(2):
                        t_acc[(gi, dh)] = p.emit(
                            "tensor", lambda te, sti=sti, gi=gi, dh=dh, pp=pp:
                            te.matmul(
                                pbk[gi * 2 + dh][:, :],
                                Psm[pp][:, :],
                                yst[sti % 2][:, dh * 512:dh * 512 + 512],
                                start=(sti == 0), stop=(sti == NST - 1)),
                            waits=[t_pm, t_yl], inc="cP")
                    t_psrd[pp] = t_acc[(gi, 1)]
                    t_yrd[bkl] = t_acc[(gi, 1)]
            for gi, tt in enumerate(grp):
                bk = tt % 2
                t_ev = {}
                for dh in range(2):
                    t_ev[dh] = p.emit("vector", lambda v, gi=gi, dh=dh:
                                      v.tensor_copy(
                        ytok[:, dh * 512:dh * 512 + 512],
                        pbk[gi * 2 + dh][:, :]),
                        waits=[t_acc[(gi, dh)], p.tok("cA"), p.tok("cV"),
                               p.tok("dO")])
                if rnd < 3:
                    p.emit("vector", lambda v, tt=tt: v.tensor_scalar(
                        vtiles[:, tt * 1024:(tt + 1) * 1024], ytok[:, :],
                        gaincol[:, tt:tt + 1], None, ALU.mult),
                        waits=[t_ev[0], t_ev[1]])
                else:
                    t_x = p.dma("sync", xt[bk][:, :],
                                x[tt * 128:(tt + 1) * 128, :], sem="dX",
                                waits=[p.tok("cV"), p.tok("dO")])
                    t_gh = p.emit("vector", lambda v, tt=tt: v.tensor_scalar(
                        cols[:, 7:8], gaincol[:, tt:tt + 1], 0.5, None,
                        ALU.mult), waits=[t_ev[0], t_ev[1]])
                    t_o = p.emit("vector", lambda v, bk=bk:
                                 v.scalar_tensor_tensor(
                                     ytok[:, :], ytok[:, :], cols[:, 7:8],
                                     xt[bk][:, :], ALU.mult, ALU.add),
                                 waits=[t_gh, t_x])
                    p.dma("sync", xo[tt * 128:(tt + 1) * 128, :], ytok[:, :],
                          sem="dO", waits=[t_o])


def build_ffn1():
    nc = bass.Bass("TRN2", target_bir_lowering=False)
    p = P(nc)
    io = {}
    io["x"] = p.dram("x", [TOWN, D], F32, kind="ExternalInput")
    io["routers"] = p.dram("routers", [4, D, L1_E], F32, kind="ExternalInput")
    io["w1"] = p.dram("w1", [L1_E, D, FF], F32, kind="ExternalInput")
    io["w2"] = p.dram("w2", [L1_E, FF, D], F32, kind="ExternalInput")
    io["xo"] = p.dram("xo", [TOWN, D], F32, kind="ExternalOutput")
    emit_ffn1(p, io)
    return p.finalize(out_sems=["dO"])


def emit_gdn1(p, io):
    xfull = io["xfull"]
    Wq = io["Wq"]
    Wk = io["Wk"]
    Wv = io["Wv"]
    Wg = io["Wg"]
    Wb = io["Wb"]
    Wa = io["Wa"]
    Alog = io["Alog"]
    dtb = io["dtb"]
    onw = io["onw"]
    o_out = io["o_out"]

    identb, t_idb = make_ident(p, BF16, "identb")
    xt = [p.sb(f"xt{i}", [128, D], F32) for i in range(2)]
    sq = p.sb("sq", [128, D], F32)
    st = p.sb("st", [128, 8], F32)
    u = p.sb("u", [128, D], BF16)
    uf = p.sb("uf", [128, D], F32)
    uT = p.sb("uT", [128, D], BF16)
    wsb = p.sb("wsb", [128, NF * 2048], BF16)
    wba = p.sb("wba", [128, NF * 8], BF16)
    qtok = p.sb("qtok", [128, 512], BF16)
    ktok = p.sb("ktok", [128, 512], BF16)
    vtok = p.sb("vtok", [128, 512], BF16)
    gtok = p.sb("gtok", [128, 512], BF16)
    QT = p.sb("QT", [128, 512], BF16)
    KT = p.sb("KT", [128, 512], BF16)
    ba = p.sb("ba", [128, 8], F32)
    bcols = p.sb("bcols", [128, 4], F32)
    gcols = p.sb("gcols", [128, 4], F32)
    spc = p.sb("spc", [128, 12], F32)
    Gcols = p.sb("Gcols", [128, 4], F32)
    nGcols = p.sb("nGcols", [128, 4], F32)
    Acols = p.sb("Acols", [128, 4], F32)
    nbA = p.sb("nbA", [128, 4], F32)
    Grow = p.sb("Grow", [1, 128], F32)
    Gbc = p.sb("Gbc", [128, 128], F32)
    GCc = p.sb("GCc", [128, 4], F32)
    ACc = p.sb("ACc", [128, 4], F32)
    Xm = p.sb("Xm", [128, 128], F32)
    Dk = p.sb("Dk", [128, 128], BF16)
    DqT = p.sb("DqT", [128, 128], BF16)
    Bm = p.sb("Bm", [128, 128], BF16)
    BmT = p.sb("BmT", [128, 128], BF16)
    Bp = [p.sb(f"Bp{i}", [128, 128], BF16) for i in range(6)]
    BpT = [p.sb(f"BpT{i}", [128, 128], BF16) for i in range(6)]
    bV = p.sb("bV", [128, 128], BF16)
    Rx = [p.sb(f"Rx{i}", [128, 128], BF16) for i in range(2)]
    Dlt = p.sb("Dlt", [128, 128], BF16)
    WmT = p.sb("WmT", [128, 128], BF16)
    o1 = p.sb("o1", [128, 128], BF16)
    Of = p.sb("Of", [128, 128], F32)
    Kr = p.sb("Kr", [128, 128], BF16)
    S32 = p.sb("S32", [128, 512], F32)
    Sbf = p.sb("Sbf", [128, 512], BF16)
    onwbc = p.sb("onwbc", [128, 128], F32)
    onwr = p.sb("onwr", [1, 128], F32)
    nab = p.sb("nab", [1, 8], F32)
    nabc = p.sb("nabc", [128, 8], F32)
    ones1f = p.sb("ones1f", [1, 128], F32)
    triUf = p.sb("triUf", [128, 128], F32)
    Mk = p.sb("Mk", [128, 128], F32)
    Mq = p.sb("Mq", [128, 128], F32)
    og = p.sb("og", [128, 512], BF16)
    pbk = [p.ps(f"pbk{i}", [128, 512], F32) for i in range(6)]
    ptb = [p.ps("ptb0", [128, 128], BF16), p.ps("ptb1", [128, 128], BF16)]

    t_1f = p.emit("vector", lambda v: v.memset(ones1f[:, :], 1.0))
    t_tri = p.emit("vector", lambda v: v.memset(triUf[:, :], 1.0))
    t_tri = p.emit("gpsimd", lambda g: g.affine_select(
        triUf[:, :], triUf[:, :], [[1, 128]], ALU.is_ge, 0.0,
        channel_multiplier=-1), waits=[t_tri])
    t_mk = p.emit("vector", lambda v: v.memset(Mk[:, :], 0.0))
    t_mk = p.emit("gpsimd", lambda g: g.affine_select(
        Mk[:, :], Mk[:, :], [[-1, 128]], ALU.is_gt, 1e30,
        channel_multiplier=1), waits=[t_mk])
    t_mq = p.emit("vector", lambda v: v.memset(Mq[:, :], 0.0))
    t_mq = p.emit("gpsimd", lambda g: g.affine_select(
        Mq[:, :], Mq[:, :], [[1, 128]], ALU.is_ge, -1e30,
        channel_multiplier=-1), waits=[t_mq])
    t_w = []
    for wi, wt in enumerate([Wq, Wk, Wv, Wg]):
        t_w.append(p.dma("gpsimd",
                         wsb[:, wi * NF * 512:(wi + 1) * NF * 512].rearrange(
                             "p (c e) -> p c e", c=NF),
                         wt.rearrange("(c p) e -> p c e", p=128), sem="dW"))
    t_w.append(p.dma("gpsimd", bass.AP(wba, 0, [[NF * 8, 128], [8, NF], [1, 4]]),
                     Wb.rearrange("(c p) e -> p c e", p=128), sem="dW"))
    t_w.append(p.dma("gpsimd", bass.AP(wba, 4, [[NF * 8, 128], [8, NF], [1, 4]]),
                     Wa.rearrange("(c p) e -> p c e", p=128), sem="dW"))
    t_wtok = p.tok("dW")
    t_on = p.dma("sync", onwr[:, :], onw[:, :], sem="dC1")
    t_al = p.dma("sync", nab[0:1, 0:4], Alog[:, :], sem="dC2")
    t_dt = p.dma("sync", nab[0:1, 4:8], dtb[:, :], sem="dC3")
    t_a1 = p.emit("scalar", lambda s: s.activation(nab[0:1, 0:4],
                                                   nab[0:1, 0:4], AF.Exp),
                  waits=[t_al])
    t_a2 = p.emit("vector", lambda v: v.tensor_scalar(
        nab[0:1, 0:4], nab[0:1, 0:4], -1.0, None, ALU.mult), waits=[t_a1])
    t_ob = p.emit("tensor", lambda te: te.matmul(
        pbk[0][:, 0:128], ones1f[:, :], onwr[:, :], start=True, stop=True),
        waits=[t_on, t_1f], inc="cP")
    t_ob = p.emit("vector", lambda v: v.tensor_copy(onwbc[:, :],
                                                    pbk[0][:, 0:128]),
                  waits=[t_ob])
    t_ab = p.emit("tensor", lambda te: te.matmul(
        pbk[0][:, 0:8], ones1f[:, :], nab[:, :], start=True, stop=True),
        waits=[t_a2, t_dt, t_1f, t_ob], inc="cP")
    t_ab = p.emit("vector", lambda v: v.tensor_copy(nabc[:, :], pbk[0][:, 0:8]),
                  waits=[t_ab])
    t_s0 = p.emit("vector", lambda v: v.memset(S32[:, :], 0.0))
    t_sb0 = p.emit("vector", lambda v: v.memset(Sbf[:, :], 0.0), waits=[t_s0])

    EPS_L2 = 1e-6
    for ch in range(NTB):
        bk = ch % 2
        p.fence(dma_sems=(("dDbg",) if DEBUG else ()))
        t_x = p.dma("sync", xt[bk][:, :], xfull[ch * 128:(ch + 1) * 128, :],
                    sem="dX", waits=[p.tok("cV"), p.tok("cA")])
        t_ln = ln_tile(p, uf, xt[bk], st, sq, waits=[t_x])
        t_r1 = p.emit("scalar", lambda s: s.activation(sq[:, :], uf[:, :],
                                                       AF.Square),
                      waits=[t_ln])
        t_r2 = p.emit("vector", lambda v: v.reduce_sum(st[:, 0:1], sq[:, :],
                                                       axis=AX.X), waits=[t_r1])
        t_r3 = p.emit("vector", lambda v: v.tensor_scalar(
            st[:, 1:2], st[:, 0:1], 1.0 / D, 1e-6, ALU.mult, ALU.add),
            waits=[t_r2])
        t_r4 = p.emit("scalar", lambda s: s.activation(st[:, 2:3], st[:, 1:2],
                                                       AF.Sqrt), waits=[t_r3])
        t_r5 = p.emit("vector", lambda v: v.reciprocal(st[:, 3:4], st[:, 2:3]),
                      waits=[t_r4])
        t_u = p.emit("vector", lambda v: v.tensor_scalar(
            u[:, :], uf[:, :], st[:, 3:4], None, ALU.mult), waits=[t_r5])
        t_uT = transpose_pass(p, uT, u, identb, ptb, [t_u], NF, t_idb,
                              evac="scalar")
        projs = []
        for wi, dst in enumerate([qtok, ktok, vtok, gtok]):
            def mp(te, wi=wi):
                ins = None
                for f in range(NF):
                    ins = te.matmul(pbk[wi % 2][:, :],
                                    uT[:, f * 128:(f + 1) * 128],
                                    wsb[:, wi * NF * 512 + f * 512:
                                        wi * NF * 512 + (f + 1) * 512],
                                    start=(f == 0), stop=(f == NF - 1))
                return ins
            t_mp = p.emit("tensor", mp, waits=t_w + [t_uT[-1], p.tok("cV"),
                                                     p.tok("cA")], inc="cP")
            projs.append(p.emit("vector", lambda v, wi=wi, dst=dst:
                                v.tensor_copy(dst[:, :], pbk[wi % 2][:, :]),
                                waits=[t_mp]))
        def mba(te):
            ins = None
            for f in range(NF):
                ins = te.matmul(pbk[2][:, 0:8], uT[:, f * 128:(f + 1) * 128],
                                wba[:, f * 8:(f + 1) * 8],
                                start=(f == 0), stop=(f == NF - 1))
            return ins
        t_mb = p.emit("tensor", mba, waits=t_w + [t_uT[-1], projs[2]],
                      inc="cP")
        t_ba = p.emit("vector", lambda v: v.tensor_copy(ba[:, :],
                                                        pbk[2][:, 0:8]),
                      waits=[t_mb])
        t_be = p.emit("scalar", lambda s: s.activation(bcols[:, :],
                                                       ba[:, 0:4], AF.Sigmoid),
                      waits=[t_ba])
        t_g1 = p.emit("vector", lambda v: v.tensor_tensor(
            spc[:, 0:4], ba[:, 4:8], nabc[:, 4:8], ALU.add),
            waits=[t_ba, t_ab])
        t_g2 = p.emit("scalar", lambda s: s.activation(spc[:, 4:8],
                                                       spc[:, 0:4], AF.Abs),
                      waits=[t_g1])
        t_g3 = p.emit("scalar", lambda s: s.activation(spc[:, 4:8],
                                                       spc[:, 4:8], AF.Exp,
                                                       scale=-1.0),
                      waits=[t_g2])
        t_g4 = p.emit("scalar", lambda s: s.activation(spc[:, 4:8],
                                                       spc[:, 4:8], AF.Ln,
                                                       bias=1.0), waits=[t_g3])
        t_g5 = p.emit("vector", lambda v: v.tensor_scalar(
            spc[:, 8:12], spc[:, 0:4], 0.0, None, ALU.max), waits=[t_g2])
        t_g6 = p.emit("vector", lambda v: v.tensor_tensor(
            spc[:, 8:12], spc[:, 8:12], spc[:, 4:8], ALU.add),
            waits=[t_g4, t_g5])
        t_gd = p.emit("vector", lambda v: v.tensor_tensor(
            gcols[:, :], spc[:, 8:12], nabc[:, 0:4], ALU.mult), waits=[t_g6])
        t_sq_ = p.emit("scalar", lambda s: s.activation(qtok[:, :], qtok[:, :],
                                                        AF.Silu),
                       waits=[projs[0]])
        t_sk = p.emit("scalar", lambda s: s.activation(ktok[:, :], ktok[:, :],
                                                       AF.Silu),
                      waits=[projs[1]])
        t_sv = p.emit("scalar", lambda s: s.activation(vtok[:, :], vtok[:, :],
                                                       AF.Silu),
                      waits=[projs[2]])
        norm_toks = []
        for hl in range(4):
            for (src, t_src, scl) in ((qtok, t_sq_, 128.0 ** -0.5),
                                      (ktok, t_sk, 1.0)):
                t_n1 = p.emit("scalar", lambda s, src=src, hl=hl: s.activation(
                    sq[:, 0:128], src[:, hl * 128:(hl + 1) * 128], AF.Square),
                    waits=[t_src, p.tok("cV")])
                t_n2 = p.emit("vector", lambda v: v.reduce_sum(
                    st[:, 4:5], sq[:, 0:128], axis=AX.X), waits=[t_n1])
                t_n3 = p.emit("vector", lambda v: v.tensor_scalar(
                    st[:, 4:5], st[:, 4:5], EPS_L2, None, ALU.add),
                    waits=[t_n2])
                t_n4 = p.emit("scalar", lambda s: s.activation(
                    st[:, 5:6], st[:, 4:5], AF.Sqrt), waits=[t_n3])
                t_n5 = p.emit("vector", lambda v: v.reciprocal(
                    st[:, 6:7], st[:, 5:6]), waits=[t_n4])
                if scl != 1.0:
                    t_n5 = p.emit("vector", lambda v, scl=scl: v.tensor_scalar(
                        st[:, 6:7], st[:, 6:7], scl, None, ALU.mult),
                        waits=[t_n5])
                norm_toks.append(p.emit(
                    "vector", lambda v, src=src, hl=hl: v.tensor_scalar(
                        src[:, hl * 128:(hl + 1) * 128],
                        src[:, hl * 128:(hl + 1) * 128],
                        st[:, 6:7], None, ALU.mult), waits=[t_n5]))
        t_qT = transpose_pass(p, QT, qtok, identb, ptb,
                              norm_toks + [p.tok("cA")], 4, t_idb,
                              evac="scalar")
        t_kT = transpose_pass(p, KT, ktok, identb, ptb, [t_qT[-1]], 4, t_idb,
                              evac="scalar")
        t_G = p.emit("tensor", lambda te: te.matmul(
            pbk[3][:, 0:4], triUf[:, :], gcols[:, :], start=True, stop=True),
            waits=[t_gd, t_tri, p.tok("cV")], inc="cP")
        t_G = p.emit("vector", lambda v: v.tensor_copy(Gcols[:, :],
                                                       pbk[3][:, 0:4]),
                     waits=[t_G])
        t_nG = p.emit("vector", lambda v: v.tensor_scalar(
            nGcols[:, :], Gcols[:, :], -1.0, None, ALU.mult), waits=[t_G])
        t_A = p.emit("scalar", lambda s: s.activation(Acols[:, :], Gcols[:, :],
                                                      AF.Exp), waits=[t_G])
        t_nbA = p.emit("vector", lambda v: v.tensor_tensor(
            nbA[:, :], bcols[:, :], Acols[:, :], ALU.mult), waits=[t_A, t_be])
        t_nbA = p.emit("vector", lambda v: v.tensor_scalar(
            nbA[:, :], nbA[:, :], -1.0, None, ALU.mult), waits=[t_nbA])
        for hl in range(4):
            h0 = hl * 128
            t_gr = p.emit("tensor", lambda te, hl=hl: te.matmul(
                pbk[3][0:1, 0:128], gcols[:, hl:hl + 1], triUf[:, :],
                start=True, stop=True), waits=[t_gd, p.tok("cV")], inc="cP")
            t_gr = p.emit("vector", lambda v: v.tensor_copy(
                Grow[:, :], pbk[3][0:1, 0:128]), waits=[t_gr])
            t_gb = p.emit("tensor", lambda te: te.matmul(
                pbk[3][:, 0:128], ones1f[:, :], Grow[:, :],
                start=True, stop=True), waits=[t_gr, t_1f], inc="cP")
            t_gb = p.emit("vector", lambda v: v.tensor_copy(
                Gbc[:, :], pbk[3][:, 0:128]), waits=[t_gb])
            t_x1 = p.emit("vector", lambda v, hl=hl: v.tensor_scalar(
                Xm[:, :], Gbc[:, :], nGcols[:, hl:hl + 1], None, ALU.add),
                waits=[t_gb, t_nG])
            t_dk = p.emit("vector", lambda v: v.tensor_tensor(
                sq[:, 0:128], Xm[:, :], Mk[:, :], ALU.add),
                waits=[t_x1, t_mk, p.tok("cA")])
            t_dk = p.emit("scalar", lambda s: s.activation(
                Dk[:, :], sq[:, 0:128], AF.Exp, scale=-1.0), waits=[t_dk])
            t_dq = p.emit("vector", lambda v: v.tensor_tensor(
                sq[:, 128:256], Xm[:, :], Mq[:, :], ALU.add),
                waits=[t_x1, t_mq, t_dk])
            t_dq = p.emit("scalar", lambda s: s.activation(
                DqT[:, :], sq[:, 128:256], AF.Exp), waits=[t_dq])
            t_kk = p.emit("tensor", lambda te, h0=h0: te.matmul(
                pbk[0][:, 0:128], KT[:, h0:h0 + 128], KT[:, h0:h0 + 128],
                start=True, stop=True), waits=[t_kT[-1], p.tok("cV"),
                                               p.tok("cA")], inc="cP")
            t_bm = p.emit("vector", lambda v, hl=hl: v.scalar_tensor_tensor(
                Bm[:, :], pbk[0][:, 0:128], bcols[:, hl:hl + 1], Dk[:, :],
                ALU.mult, ALU.mult), waits=[t_kk, t_dk, t_be])
            t_bmT = p.emit("tensor", lambda te: te.transpose(
                ptb[0][:, :], Bm[:, :], identb[:, :]), waits=[t_bm, t_idb],
                inc="cP")
            t_bmT = p.emit("scalar", lambda s: s.copy(BmT[:, :], ptb[0][:, :]),
                           waits=[t_bmT])
            t_sq2 = {}
            prev, prevT = Bm[:, :], BmT[:, :]
            tok_prev, tok_prevT = t_bm, t_bmT
            for lv in range(6):
                t_p1 = p.emit("tensor", lambda te, prev=prev, prevT=prevT:
                              te.matmul(pbk[1][:, 0:128], prevT, prev,
                                        start=True, stop=True),
                              waits=[tok_prev, tok_prevT, p.tok("cA"),
                                     p.tok("cV")], inc="cP")
                tok_a = p.emit("scalar", lambda s, lv=lv: s.copy(
                    Bp[lv][:, :], pbk[1][:, 0:128]), waits=[t_p1])
                t_p2 = p.emit("tensor", lambda te, prev=prev, prevT=prevT:
                              te.matmul(pbk[2][:, 0:128], prev, prevT,
                                        start=True, stop=True),
                              waits=[tok_a], inc="cP")
                tok_b = p.emit("scalar", lambda s, lv=lv: s.copy(
                    BpT[lv][:, :], pbk[2][:, 0:128]), waits=[t_p2])
                prev, prevT = Bp[lv][:, :], BpT[lv][:, :]
                tok_prev, tok_prevT = tok_a, tok_b
            t_ks = p.emit("tensor", lambda te, h0=h0, hl=hl: te.matmul(
                pbk[0][:, 0:128], KT[:, h0:h0 + 128], Sbf[:, h0:h0 + 128],
                start=True, stop=True), waits=[t_kT[-1], t_sb0, p.tok("cV")],
                inc="cP")
            t_bv = p.emit("vector", lambda v, h0=h0, hl=hl: v.tensor_scalar(
                bV[:, :], vtok[:, h0:h0 + 128], bcols[:, hl:hl + 1], None,
                ALU.mult), waits=[t_sv, t_be] + norm_toks)
            t_R = p.emit("vector", lambda v, hl=hl: v.scalar_tensor_tensor(
                Rx[0][:, :], pbk[0][:, 0:128], nbA[:, hl:hl + 1], bV[:, :],
                ALU.mult, ALU.add), waits=[t_ks, t_bv, t_nbA])
            tok_X = t_R
            cur = 0
            for lv in range(5, -1, -1):
                t_ap = p.emit("tensor", lambda te, lv=lv, cur=cur: te.matmul(
                    pbk[1][:, 0:128], BpT[lv][:, :], Rx[cur][:, :],
                    start=True, stop=True),
                    waits=[tok_X, tok_prevT, p.tok("cV")], inc="cP")
                tok_X = p.emit("vector", lambda v, cur=cur: v.tensor_tensor(
                    Rx[1 - cur][:, :], pbk[1][:, 0:128], Rx[cur][:, :],
                    ALU.add), waits=[t_ap])
                cur = 1 - cur
            t_ap = p.emit("tensor", lambda te, cur=cur: te.matmul(
                pbk[1][:, 0:128], BmT[:, :], Rx[cur][:, :],
                start=True, stop=True), waits=[tok_X], inc="cP")
            t_dl = p.emit("vector", lambda v, cur=cur: v.scalar_tensor_tensor(
                Dlt[:, :], pbk[1][:, 0:128], -1.0, Rx[cur][:, :],
                ALU.mult, ALU.add), waits=[t_ap])
            t_kq = p.emit("tensor", lambda te, h0=h0: te.matmul(
                pbk[0][:, 0:128], KT[:, h0:h0 + 128], QT[:, h0:h0 + 128],
                start=True, stop=True), waits=[t_qT[-1], t_dl], inc="cP")
            t_wm = p.emit("vector", lambda v: v.tensor_tensor(
                WmT[:, :], pbk[0][:, 0:128], DqT[:, :], ALU.mult),
                waits=[t_kq, t_dq])
            t_qs0 = p.emit("tensor", lambda te, h0=h0: te.matmul(
                pbk[2][:, 0:128], QT[:, h0:h0 + 128], Sbf[:, h0:h0 + 128],
                start=True, stop=True), waits=[t_qT[-1], t_sb0, t_wm],
                inc="cP")
            t_o1 = p.emit("scalar", lambda s, hl=hl: s.activation(
                o1[:, :], pbk[2][:, 0:128], AF.Copy,
                scale=Acols[:, hl:hl + 1]), waits=[t_qs0, t_A])
            t_ow = p.emit("tensor", lambda te: te.matmul(
                pbk[0][:, 0:128], WmT[:, :], Dlt[:, :], start=True, stop=True),
                waits=[t_wm, t_dl, t_o1], inc="cP")
            t_of = p.emit("vector", lambda v: v.tensor_tensor(
                Of[:, :], pbk[0][:, 0:128], o1[:, :], ALU.add),
                waits=[t_ow, t_o1])
            t_gc = p.emit("scalar", lambda s, hl=hl: s.activation(
                GCc[:, hl:hl + 1], Gcols[:, hl:hl + 1], AF.Exp, scale=-1.0,
                bias=Gbc[:, 127:128]), waits=[t_gb, t_G, p.tok("cV")])
            t_ac = p.emit("scalar", lambda s, hl=hl: s.activation(
                ACc[:, hl:hl + 1], Gbc[:, 127:128], AF.Exp),
                waits=[t_gb, t_gc])
            t_kr = p.emit("vector", lambda v, h0=h0, hl=hl: v.tensor_scalar(
                Kr[:, :], ktok[:, h0:h0 + 128], GCc[:, hl:hl + 1], None,
                ALU.mult), waits=[t_gc] + norm_toks)
            t_su = p.emit("tensor", lambda te: te.matmul(
                pbk[1][:, 0:128], Kr[:, :], Dlt[:, :], start=True, stop=True),
                waits=[t_kr, t_dl, p.tok("cV")], inc="cP")
            t_s32 = p.emit("vector", lambda v, h0=h0, hl=hl:
                           v.scalar_tensor_tensor(
                               S32[:, h0:h0 + 128], S32[:, h0:h0 + 128],
                               ACc[:, hl:hl + 1], pbk[1][:, 0:128],
                               ALU.mult, ALU.add), waits=[t_su, t_ac])
            t_sb0 = p.emit("vector", lambda v, h0=h0: v.tensor_copy(
                Sbf[:, h0:h0 + 128], S32[:, h0:h0 + 128]), waits=[t_s32])
            t_q1 = p.emit("scalar", lambda s: s.activation(
                sq[:, 0:128], Of[:, :], AF.Square), waits=[t_of])
            t_q2 = p.emit("vector", lambda v: v.reduce_sum(
                st[:, 4:5], sq[:, 0:128], axis=AX.X), waits=[t_q1])
            t_q3 = p.emit("vector", lambda v: v.tensor_scalar(
                st[:, 4:5], st[:, 4:5], 1.0 / 128, 1e-6, ALU.mult, ALU.add),
                waits=[t_q2])
            t_q4 = p.emit("scalar", lambda s: s.activation(
                st[:, 5:6], st[:, 4:5], AF.Sqrt), waits=[t_q3])
            t_q5 = p.emit("vector", lambda v: v.reciprocal(
                st[:, 6:7], st[:, 5:6]), waits=[t_q4])
            t_q6 = p.emit("vector", lambda v: v.tensor_scalar(
                Of[:, :], Of[:, :], st[:, 6:7], None, ALU.mult), waits=[t_q5])
            t_q7 = p.emit("vector", lambda v: v.tensor_tensor(
                Of[:, :], Of[:, :], onwbc[:, :], ALU.mult), waits=[t_q6])
            t_gs = p.emit("scalar", lambda s, h0=h0: s.activation(
                sq[:, 0:128], gtok[:, h0:h0 + 128], AF.Silu),
                waits=[projs[3], t_q7])
            p.emit("vector", lambda v, h0=h0: v.tensor_tensor(
                og[:, h0:h0 + 128], Of[:, :], sq[:, 0:128], ALU.mult),
                waits=[t_gs, p.tok("dO")])
        p.dma("sync", o_out[ch * 128:(ch + 1) * 128, :], og[:, :], sem="dO",
              waits=[p.tok("cV")])
        if DEBUG and ch == 0:
            wd = [p.tok("cV"), p.tok("cA"), p.tok("cP"), p.tok("cG")]
            for nm, tile in [("qtok", qtok), ("ktok", ktok), ("vtok", vtok),
                             ("gcols", gcols), ("bcols", bcols),
                             ("Gcols", Gcols), ("Dk", Dk), ("DqT", DqT),
                             ("Bm", Bm), ("Dlt", Dlt), ("Of", Of),
                             ("S32", S32)]:
                dt_ = tile.dtype if hasattr(tile, "dtype") else BF16
                dd = p.dram("dbg_" + nm, list(tile.shape), dt_,
                            kind="ExternalOutput")
                p.dma("sync", dd[:, :], tile[:, :], sem="dDbg", waits=wd)
            p.emit("gpsimd", lambda g: g.wait_ge(p._sems["dDbg"],
                                                 p.cnt["dDbg"]), inc=None)


def build_gdn1():
    nc = bass.Bass("TRN2", target_bir_lowering=False)
    p = P(nc)
    io = {}
    io["xfull"] = p.dram("xfull", [S, D], F32, kind="ExternalInput")
    io["Wq"] = p.dram("Wq", [D, 512], F32, kind="ExternalInput")
    io["Wk"] = p.dram("Wk", [D, 512], F32, kind="ExternalInput")
    io["Wv"] = p.dram("Wv", [D, 512], F32, kind="ExternalInput")
    io["Wg"] = p.dram("Wg", [D, 512], F32, kind="ExternalInput")
    io["Wb"] = p.dram("Wb", [D, 4], F32, kind="ExternalInput")
    io["Wa"] = p.dram("Wa", [D, 4], F32, kind="ExternalInput")
    io["Alog"] = p.dram("Alog", [1, 4], F32, kind="ExternalInput")
    io["dtb"] = p.dram("dtb", [1, 4], F32, kind="ExternalInput")
    io["onw"] = p.dram("onw", [1, 128], F32, kind="ExternalInput")
    io["o_out"] = p.dram("o_out", [S, 512], BF16, kind="ExternalOutput")
    emit_gdn1(p, io)
    return p.finalize(out_sems=["dO"])


def emit_gdn2(p, io):
    x = io["x"]
    o_own = io["o_own"]
    Wo = io["Wo"]
    Wfc = io["Wfc"]
    Wpj = io["Wpj"]
    xo = io["xo"]

    identb, t_idb = make_ident(p, BF16, "identb")
    xt = [p.sb(f"xt{i}", [128, D], F32) for i in range(2)]
    sq = p.sb("sq", [128, D], F32)
    st = p.sb("st", [128, 8], F32)
    osb = p.sb("osb", [128, D], BF16)
    oT = p.sb("oT", [128, D], BF16)
    m = p.sb("m", [128, D], BF16)
    wosb = p.sb("wosb", [128, NF * D], BF16)
    wfsb = p.sb("wfsb", [128, NF * FF // 2], BF16)
    wpsb = p.sb("wpsb", [128, NFF * D], BF16)
    x1g = p.sb("x1g", [128, 4 * D], F32)
    xlnt = p.sb("xlnt", [128, D], F32)
    mTg = p.sb("mTg", [128, 4 * D], BF16)
    h2T = p.sb("h2T", [128, NFF * 512], BF16)
    yout = p.sb("yout", [128, D], F32)
    pbk = [p.ps(f"pbk{i}", [128, 512], F32) for i in range(6)]
    ptb = [p.ps("ptb0", [128, 128], BF16), p.ps("ptb1", [128, 128], BF16)]

    t_wo = p.dma("gpsimd", wosb[:, :].rearrange("p (c d) -> p c d", c=NF),
                 Wo.rearrange("(c p) d -> p c d", p=128), sem="dWo")
    t_wp = p.dma("gpsimd", wpsb[:, :].rearrange("p (c d) -> p c d", c=NFF),
                 Wpj.rearrange("(c p) d -> p c d", p=128), sem="dWp")
    for grp in range(4):
        p.fence(dma_sems=("dXo",))
        for lt in range(4):
            t = grp * 4 + lt
            bk = lt % 2
            p.fence()
            t_x = p.dma("sync", xt[bk][:, :], x[t * 128:(t + 1) * 128, :],
                        sem="dX", waits=[p.tok("cV"), p.tok("cA")])
            t_ln = ln_tile(p, xlnt, xt[bk], st, sq, waits=[t_x])
            t_ln2 = p.emit("vector", lambda v, lt=lt: v.tensor_copy(
                x1g[:, lt * D:(lt + 1) * D], xlnt[:, :]), waits=[t_ln])
            t_o = p.dma("sync", osb[:, :], o_own[t * 128:(t + 1) * 128, :],
                        sem="dOin", waits=[p.tok("cP")])
            t_oT = transpose_pass(p, oT, osb, identb, ptb,
                                  [t_o, t_ln2], NF, t_idb, evac="scalar")
            for dh in range(2):
                def mo(te, dh=dh):
                    ins = None
                    for f in range(NF):
                        ins = te.matmul(
                            pbk[dh][:, :], oT[:, f * 128:(f + 1) * 128],
                            wosb[:, f * D + dh * 512:f * D + dh * 512 + 512],
                            start=(f == 0), stop=(f == NF - 1))
                    return ins
                t_mo = p.emit("tensor", mo, waits=[t_wo, t_oT[-1],
                                                   p.tok("cV")], inc="cP")
                p.emit("vector", lambda v, lt=lt, dh=dh: v.tensor_tensor(
                    x1g[:, lt * D + dh * 512:lt * D + dh * 512 + 512],
                    pbk[dh][:, :],
                    x1g[:, lt * D + dh * 512:lt * D + dh * 512 + 512],
                    ALU.add), waits=[t_mo])
            t_m1 = p.emit("scalar", lambda s, lt=lt: s.activation(
                sq[:, :], x1g[:, lt * D:(lt + 1) * D], AF.Square),
                waits=[p.tok("cV")])
            t_m2 = p.emit("vector", lambda v: v.reduce_sum(
                st[:, 0:1], sq[:, :], axis=AX.X), waits=[t_m1])
            t_m3 = p.emit("vector", lambda v: v.tensor_scalar(
                st[:, 1:2], st[:, 0:1], 1.0 / D, 1e-6, ALU.mult, ALU.add),
                waits=[t_m2])
            t_m4 = p.emit("scalar", lambda s: s.activation(
                st[:, 2:3], st[:, 1:2], AF.Sqrt), waits=[t_m3])
            t_m5 = p.emit("vector", lambda v: v.reciprocal(
                st[:, 3:4], st[:, 2:3]), waits=[t_m4])
            t_m6 = p.emit("vector", lambda v, lt=lt: v.tensor_scalar(
                m[:, :], x1g[:, lt * D:(lt + 1) * D], st[:, 3:4], None,
                ALU.mult), waits=[t_m5])
            transpose_pass(p, mTg, m, identb, ptb, [t_m6], NF, t_idb,
                           evac="scalar", dst_col0=lt * D)
        FFH = FF // 2
        t_y = {}
        for ffh in range(2):
            t_wf = p.dma("gpsimd",
                         wfsb[:, :].rearrange("p (c f) -> p c f", c=NF),
                         Wfc[:, ffh * FFH:(ffh + 1) * FFH].rearrange(
                             "(c p) f -> p c f", p=128), sem="dWf",
                         waits=[p.tok("cP")])
            for ffl in range(NFF // 2):
                ffb = ffh * (NFF // 2) + ffl
                bk = ffb % 2

                def mf(te, ffl=ffl, bk=bk):
                    ins = None
                    for f in range(NF):
                        ins = te.matmul(
                            pbk[bk][:, :],
                            wfsb[:, f * FFH + ffl * 128:
                                 f * FFH + (ffl + 1) * 128],
                            mTg[:, f * 128:(f + 1) * 128].rearrange(
                                "p c -> p c") if False else
                            bass.AP(mTg, f * 128,
                                    [[4 * D, 128], [D, 4], [1, 128]]),
                            start=(f == 0), stop=(f == NF - 1))
                    return ins
                t_mf = p.emit("tensor", mf, waits=[t_wf, p.tok("cA"),
                                                   p.tok("cV")], inc="cP")
                t_rl = p.emit("vector", lambda v, bk=bk: v.tensor_scalar(
                    sq[:, 0:512], pbk[bk][:, :], 0.0, None, ALU.max),
                    waits=[t_mf])
                p.emit("scalar", lambda s, ffb=ffb: s.activation(
                    h2T[:, ffb * 512:(ffb + 1) * 512], sq[:, 0:512],
                    AF.Square), waits=[t_rl])
        for lt in range(4):
            t = grp * 4 + lt
            bk = lt % 2
            t_x2 = p.dma("sync", xt[bk][:, :], x[t * 128:(t + 1) * 128, :],
                         sem="dX", waits=[p.tok("cV"), p.tok("cA")])
            for dh in range(2):
                def mp2(te, lt=lt, dh=dh):
                    ins = None
                    for ffb in range(NFF):
                        ins = te.matmul(
                            pbk[4 + dh][:, :],
                            h2T[:, ffb * 512 + lt * 128:
                                ffb * 512 + (lt + 1) * 128],
                            wpsb[:, ffb * D + dh * 512:
                                 ffb * D + dh * 512 + 512],
                            start=(ffb == 0), stop=(ffb == NFF - 1))
                    return ins
                t_mp = p.emit("tensor", mp2, waits=[t_wp, p.tok("cA"),
                                                    p.tok("cV")], inc="cP")
                t_ya = p.emit("vector", lambda v, lt=lt, dh=dh:
                              v.tensor_tensor(
                                  yout[:, dh * 512:dh * 512 + 512],
                                  pbk[4 + dh][:, :],
                                  x1g[:, lt * D + dh * 512:
                                      lt * D + dh * 512 + 512], ALU.add),
                              waits=[t_mp, p.tok("dXo")])
                t_y[dh] = p.emit("vector", lambda v, dh=dh, bk=bk:
                                 v.tensor_tensor(
                                     yout[:, dh * 512:dh * 512 + 512],
                                     yout[:, dh * 512:dh * 512 + 512],
                                     xt[bk][:, dh * 512:dh * 512 + 512],
                                     ALU.add), waits=[t_ya, t_x2])
            p.dma("sync", xo[t * 128:(t + 1) * 128, :], yout[:, :], sem="dXo",
                  waits=[t_y[0], t_y[1]])



CC8 = [[0, 1, 2, 3, 4, 5, 6, 7]]
CCPAIR = [[0, 1], [2, 3], [4, 5], [6, 7]]


def emit_osel(p, io):
    obuf, invq, o_own = io["obuf"], io["invq"], io["o_own"]
    invc = p.sb("invc", [128, NTB], F32)
    tIot = p.sb("tIot", [128, 128], F32)
    Psm = [p.sb(f"Psm{i}", [128, 128], BF16) for i in range(2)]
    ob = [p.sb(f"ob{i}", [128, D], BF16) for i in range(2)]
    osl = p.sb("osl", [128, D], BF16)
    pbk = [p.ps(f"pbk{i}", [128, 512], F32) for i in range(6)]
    t_iv = p.dma("sync", invc[:, :], invq[:, :], sem="dIv")
    t_ti = p.emit("gpsimd", lambda g: g.iota(
        tIot[:, :], [[1, 128]], base=0, channel_multiplier=0,
        allow_small_or_imprecise_dtypes=True))
    for g0 in range(0, NT, 3):
        tts = list(range(g0, min(g0 + 3, NT)))
        p.fence(dma_sems=("dOw", "dOb"))
        t_rd = {0: None, 1: None}
        t_psrd = {0: None, 1: None}
        t_lastmm = {}
        t_obp = {0: None, 1: None}
        for st in range(NTB):
            bk = st % 2
            qe = "sync" if bk == 0 else "scalar"
            sm = "dOb" if bk == 0 else "dOb2"
            t_o1 = p.dma(qe, ob[bk][:, 0:512],
                         obuf[st * 128:(st + 1) * 128, :], sem=sm,
                         waits=[t_obp[bk], t_rd[bk]])
            t_o2 = p.dma(qe, ob[bk][:, 512:1024],
                         obuf[S + st * 128:S + (st + 1) * 128, :], sem=sm,
                         waits=[t_o1])
            t_obp[bk] = t_o2
            for i, tt in enumerate(tts):
                pb = (st * len(tts) + i) % 2
                t_ps = p.emit("vector", lambda v, tt=tt, st=st, pb=pb:
                              v.tensor_scalar(
                                  Psm[pb][:, :], tIot[:, :], float(tt * 128),
                                  invc[:, st:st + 1], ALU.add, ALU.is_equal),
                              waits=[t_iv, t_ti] +
                                    ([t_psrd[pb]] if t_psrd[pb] else []))

                def mm(te, i=i, pb=pb, bk=bk, fi=(st == 0),
                       la=(st == NTB - 1)):
                    te.matmul(pbk[2 * i][:, :], Psm[pb][:, :],
                              ob[bk][:, 0:512], start=fi, stop=la)
                    return te.matmul(pbk[2 * i + 1][:, :], Psm[pb][:, :],
                                     ob[bk][:, 512:1024], start=fi, stop=la)
                t_mm = p.emit("tensor", mm, waits=[t_ps, t_o2], inc="cP")
                t_psrd[pb] = t_mm
                t_rd[bk] = t_mm
                t_lastmm[i] = t_mm
        for i, tt in enumerate(tts):
            t_e0 = p.emit("scalar", lambda sc, i=i: sc.copy(
                osl[:, 0:512], pbk[2 * i][:, :]),
                waits=[t_lastmm[i], p.tok("dOw")])
            t_e1 = p.emit("scalar", lambda sc, i=i: sc.copy(
                osl[:, 512:1024], pbk[2 * i + 1][:, :]),
                waits=[t_lastmm[i], p.tok("dOw")])
            p.dma("sync", o_own[tt * 128:(tt + 1) * 128, :], osl[:, :],
                  sem="dOw", waits=[t_e0, t_e1])


def emit_cc(p, kind, groups, src, dst, waits=()):
    return p.emit("gpsimd", lambda g: g.collective_compute(
        kind, ALU.bypass, groups, [src], [dst]),
        waits=waits, inc="cG", inc_by=1)


def emit_permute_glob(p, dst, src, waits):
    dstv = dst.rearrange("(b j q pp) one -> q b j (pp one)",
                         b=B, j=16, q=2, pp=128)
    srcv = src.rearrange("(b q i pp) one -> q b i (pp one)",
                         b=B, q=2, i=16, pp=128)
    toks = []
    for par in range(2):
        toks.append(p.dma("sync", dstv[par], srcv[par], sem="dPm",
                          waits=list(waits) + toks[-1:]))
    return toks


def emit_permute_pair(p, dst, src, waits):
    Dw = dst.shape[1]
    dstv = dst.rearrange("(i q pp) d -> q i pp d", i=16, q=2, pp=128)
    srcv = src.rearrange("(q i pp) d -> q i pp d", q=2, i=16, pp=128)
    toks = []
    for par in range(2):
        toks.append(p.dma("sync", dstv[par], srcv[par], sem="dPm",
                          waits=list(waits) + toks[-1:]))
    return toks


def build_mega():
    nc = bass.Bass("TRN2", target_bir_lowering=False)
    p = P(nc)

    def ein(n, sh, dt):
        return p.dram(n, sh, dt, kind="ExternalInput")
    x0 = ein("x", [TOWN, D], F32)
    posq = ein("posq", [TOWN, 1], F32)
    owntiles = ein("owntiles", [1, NT], F32)
    invq = ein("invq", [128, NTB], F32)
    rw = ein("rw", [D, L0_E], F32)
    f0w1 = ein("f0w1", [L0_E, D, FF], F32)
    f0b1 = ein("f0b1", [L0_E, FF], F32)
    f0w2 = ein("f0w2", [L0_E, FF, D], F32)
    f0b2 = ein("f0b2", [L0_E, D], F32)
    w_inT = ein("w_inT", [D, 3 * D], F32)
    w_outT = ein("w_outT", [D, D], F32)
    routers = ein("routers", [4, D, L1_E], F32)
    f1w1 = ein("f1w1", [L1_E, D, FF], F32)
    f1w2 = ein("f1w2", [L1_E, FF, D], F32)
    gw = {n: ein(n, [D, 512], F32) for n in ("Wq", "Wk", "Wv", "Wg")}
    gw["Wb"] = ein("Wb", [D, 4], F32)
    gw["Wa"] = ein("Wa", [D, 4], F32)
    gw["Alog"] = ein("Alog", [1, 4], F32)
    gw["dtb"] = ein("dtb", [1, 4], F32)
    gw["onw"] = ein("onw", [1, 128], F32)
    Wo = ein("Wo", [D, D], F32)
    Wfc = ein("Wfc", [D, FF], F32)
    Wpj = ein("Wpj", [FF, D], F32)
    out = p.dram("out", [TOWN, D], F32, kind="ExternalOutput")

    xa = p.dram("xa", [TOWN, D], F32)
    xb = p.dram("xb", [TOWN, D], F32)
    xc = p.dram("xc", [TOWN, D], F32)
    xd = p.dram("xd", [TOWN, D], F32)
    xe = p.dram("xe", [TOWN, D], F32)
    idxs = [p.dram(f"idxs{i}", [TOWN, 1], F32) for i in range(2)]
    ogt = [p.dram(f"ogt{i}", [TOWN, 1], F32) for i in range(2)]
    idxrm = [p.dram(f"idxrm{i}", [T, 1], F32) for i in range(2)]
    idxg = [p.dram(f"idxg{i}", [T, 1], F32) for i in range(2)]
    xprm = [p.dram(f"xprm{i}", [2 * TOWN, D], F32) for i in range(2)]
    xfullB = p.dram("xfullB", [S, D], F32)
    xfullE = p.dram("xfullE", [S, D], F32)
    oout = p.dram("oout", [S, 512], BF16)
    obuf = p.dram("obuf", [2 * S, 512], BF16)
    oown = p.dram("oown", [TOWN, D], BF16)

    def phase(ns, fn, *a):
        p.ns = ns
        m = p.mark()
        fn(*a)
        p.phase_end()
        p.release(m)

    def cc_idx(k):
        t = emit_cc(p, "AllGather", CC8, idxs[k][:, :], idxrm[k][:, :])
        emit_permute_glob(p, idxg[k], idxrm[k], [t])

    def cc_xfull(src, k, dst):
        t = emit_cc(p, "AllGather", CCPAIR, src[:, :], xprm[k][:, :])
        emit_permute_pair(p, dst, xprm[k], [t])

    def cc_o():
        emit_cc(p, "AllGather", CCPAIR, oout[:, :], obuf[:, :])

    phase("r1.", emit_route0, p,
          {"x": x0, "rw": rw, "o_idx": idxs[0], "o_gate": ogt[0]})
    phase("c1.", cc_idx, 0)
    phase("a.", emit_ffn0, p,
          {"x": x0, "idxg": idxg[0], "oidx": idxs[0], "ogate": ogt[0],
           "owntiles": owntiles, "w1": f0w1, "b1": f0b1, "w2": f0w2,
           "b2": f0b2, "xo": xa})
    phase("c2.", cc_xfull, xa, 0, xfullB)
    phase("b.", emit_attn0, p,
          {"xfull": xfullB, "xq": xa, "posq": posq, "w_inT": w_inT,
           "w_outT": w_outT, "xo": xb})
    phase("r2.", emit_route0, p,
          {"x": xb, "rw": rw, "o_idx": idxs[1], "o_gate": ogt[1]})
    phase("c3.", cc_idx, 1)
    phase("c.", emit_ffn0, p,
          {"x": xb, "idxg": idxg[1], "oidx": idxs[1], "ogate": ogt[1],
           "owntiles": owntiles, "w1": f0w1, "b1": f0b1, "w2": f0w2,
           "b2": f0b2, "xo": xc})
    phase("d.", emit_ffn1, p,
          {"x": xc, "routers": routers, "w1": f1w1, "w2": f1w2, "xo": xd})
    phase("c4.", cc_xfull, xd, 1, xfullE)
    phase("e.", emit_gdn1, p, dict(gw, xfull=xfullE, o_out=oout))
    phase("c5.", cc_o)
    phase("o.", emit_osel, p, {"obuf": obuf, "invq": invq, "o_own": oown})
    phase("f.", emit_gdn2, p,
          {"x": xd, "o_own": oown, "Wo": Wo, "Wfc": Wfc, "Wpj": Wpj,
           "xo": xe})
    phase("g.", emit_ffn1, p,
          {"x": xe, "routers": routers, "w1": f1w1, "w2": f1w2, "xo": out})
    return p.finalize(out_sems=["g.dO"])




def build_gdn2():
    nc = bass.Bass("TRN2", target_bir_lowering=False)
    p = P(nc)
    io = {}
    io["x"] = p.dram("x", [TOWN, D], F32, kind="ExternalInput")
    io["o_own"] = p.dram("o_own", [TOWN, D], BF16, kind="ExternalInput")
    io["Wo"] = p.dram("Wo", [D, D], F32, kind="ExternalInput")
    io["Wfc"] = p.dram("Wfc", [D, FF], F32, kind="ExternalInput")
    io["Wpj"] = p.dram("Wpj", [FF, D], F32, kind="ExternalInput")
    io["xo"] = p.dram("xo", [TOWN, D], F32, kind="ExternalOutput")
    emit_gdn2(p, io)
    return p.finalize(out_sems=["dXo"])


def own_rows(core):
    b, h = core // 2, core % 2
    return np.concatenate([np.arange(b * S + (2 * i + h) * 128,
                                     b * S + (2 * i + h) * 128 + 128)
                           for i in range(NT)])


_OWN = [own_rows(c) for c in range(8)]


def shard_own(x_flat):
    return [np.ascontiguousarray(x_flat[_OWN[c]]) for c in range(8)]


def unshard_own(parts):
    sh = (T,) + tuple(parts[0].shape[1:])
    out = np.empty(sh, parts[0].dtype)
    for c in range(8):
        out[_OWN[c]] = parts[c]
    return out


_CACHE = {}
_OWNT = [np.ascontiguousarray(
    np.array([[c // 2 * NTB + 2 * i + c % 2 for i in range(NT)]],
             dtype=np.float32)) for c in range(8)]
_POSQ = [np.ascontiguousarray(
    (own_rows(c) - (c // 2) * S).astype(np.float32).reshape(-1, 1))
    for c in range(8)]


def _get(name, fn):
    if name not in _CACHE:
        _CACHE[name] = fn()
    return _CACHE[name]


def run8(nc, per_core, shared):
    ins = [dict(shared, **pc) for pc in per_core]
    res = run_bass_kernel_spmd(nc, ins, core_ids=list(range(8)))
    return res.results


def step_ffn0(x_flat, I):
    xs = shard_own(x_flat)
    nc1 = _get("route0", build_route0)
    r1 = run8(nc1, [{"x": xs[c]} for c in range(8)],
              {"rw": I["L0_ffn_router_w"]})
    idx_parts = [np.asarray(r1[c]["o_idx"]) for c in range(8)]
    gate_parts = [np.asarray(r1[c]["o_gate"]) for c in range(8)]
    idxg = unshard_own(idx_parts)
    nc2 = _get("ffn0", build_ffn0)
    r2 = run8(nc2, [{"x": xs[c], "oidx": idx_parts[c], "ogate": gate_parts[c],
                     "owntiles": _OWNT[c]} for c in range(8)],
              {"idxg": idxg, "w1": I["L0_ffn_w1"], "b1": I["L0_ffn_b1"],
               "w2": I["L0_ffn_w2"], "b2": I["L0_ffn_b2"]})
    return unshard_own([np.asarray(r2[c]["xo"]) for c in range(8)])


def step_attn0(x_flat, I):
    xs = shard_own(x_flat)
    w_inT = _get("w_inT", lambda: np.ascontiguousarray(
        I["L0_attn_w_in"].astype(np.float32).T))
    w_outT = _get("w_outT", lambda: np.ascontiguousarray(
        I["L0_attn_w_out"].astype(np.float32).T))
    xfulls = [np.ascontiguousarray(x_flat[b * S:(b + 1) * S]) for b in range(4)]
    nc = _get("attn0", build_attn0)
    r = run8(nc, [{"xfull": xfulls[c // 2], "xq": xs[c], "posq": _POSQ[c]}
                  for c in range(8)],
             {"w_inT": w_inT, "w_outT": w_outT})
    return unshard_own([np.asarray(r[c]["xo"]) for c in range(8)])


def step_ffn1(x_flat, I):
    xs = shard_own(x_flat)
    nc = _get("ffn1", build_ffn1)
    r = run8(nc, [{"x": xs[c]} for c in range(8)],
             {"routers": I["L1_ffn_routers"], "w1": I["L1_ffn_W1"],
              "w2": I["L1_ffn_W2"]})
    return unshard_own([np.asarray(r[c]["xo"]) for c in range(8)])


def step_gdn(x_flat, I):
    xs = shard_own(x_flat)
    xfulls = [np.ascontiguousarray(x_flat[b * S:(b + 1) * S]) for b in range(4)]
    sl = {}
    for par in range(2):
        hc = slice(par * 512, par * 512 + 512)
        hs = slice(par * 4, par * 4 + 4)
        sl[par] = {
            "Wq": _get(f"Wq{par}", lambda hc=hc: np.ascontiguousarray(
                I["L1_attn_Wq"][:, hc])),
            "Wk": _get(f"Wk{par}", lambda hc=hc: np.ascontiguousarray(
                I["L1_attn_Wk"][:, hc])),
            "Wv": _get(f"Wv{par}", lambda hc=hc: np.ascontiguousarray(
                I["L1_attn_Wv"][:, hc])),
            "Wg": _get(f"Wg{par}", lambda hc=hc: np.ascontiguousarray(
                I["L1_attn_Wg"][:, hc])),
            "Wb": _get(f"Wb{par}", lambda hs=hs: np.ascontiguousarray(
                I["L1_attn_Wb"][:, hs])),
            "Wa": _get(f"Wa{par}", lambda hs=hs: np.ascontiguousarray(
                I["L1_attn_Wa"][:, hs])),
            "Alog": _get(f"Al{par}", lambda hs=hs: np.ascontiguousarray(
                I["L1_attn_A_log"][None, hs])),
            "dtb": _get(f"dt{par}", lambda hs=hs: np.ascontiguousarray(
                I["L1_attn_dt_bias"][None, hs])),
        }
    onw = _get("onw", lambda: np.ascontiguousarray(
        I["L1_attn_o_norm_w"][None, :].astype(np.float32)))
    nc1 = _get("gdn1", build_gdn1)
    r1 = run8(nc1, [dict(sl[c % 2], xfull=xfulls[c // 2]) for c in range(8)],
              {"onw": onw})
    o_parts = [np.asarray(r1[c]["o_out"]) for c in range(8)]
    import ml_dtypes
    o_own = []
    for c in range(8):
        b = c // 2
        ofull = np.concatenate([o_parts[2 * b], o_parts[2 * b + 1]], axis=1)
        o_own.append(np.ascontiguousarray(ofull[_OWN[c] - b * S]))
    nc2 = _get("gdn2", build_gdn2)
    r2 = run8(nc2, [{"x": xs[c], "o_own": o_own[c]} for c in range(8)],
              {"Wo": I["L1_attn_Wo"], "Wfc": I["L1_attn_W_fc"],
               "Wpj": I["L1_attn_W_proj"]})
    return unshard_own([np.asarray(r2[c]["xo"]) for c in range(8)])


_INVQ = []
for _c in range(8):
    _iv = np.full(S, 2.0e9, np.float32)
    _iv[_OWN[_c] - (_c // 2) * S] = np.arange(TOWN, dtype=np.float32)
    _INVQ.append(np.ascontiguousarray(_iv.reshape(NTB, 128).T))


def kernel(**inputs):
    I = {k: np.asarray(v) for k, v in inputs.items()}
    x = np.ascontiguousarray(I["L0_attn_x"].reshape(T, D).astype(np.float32))
    xs = shard_own(x)
    w_inT = _get("w_inT", lambda: np.ascontiguousarray(
        I["L0_attn_w_in"].astype(np.float32).T))
    w_outT = _get("w_outT", lambda: np.ascontiguousarray(
        I["L0_attn_w_out"].astype(np.float32).T))
    sl = {}
    for par in range(2):
        hc = slice(par * 512, par * 512 + 512)
        hs = slice(par * 4, par * 4 + 4)
        sl[par] = {
            "Wq": _get(f"Wq{par}", lambda hc=hc: np.ascontiguousarray(
                I["L1_attn_Wq"][:, hc])),
            "Wk": _get(f"Wk{par}", lambda hc=hc: np.ascontiguousarray(
                I["L1_attn_Wk"][:, hc])),
            "Wv": _get(f"Wv{par}", lambda hc=hc: np.ascontiguousarray(
                I["L1_attn_Wv"][:, hc])),
            "Wg": _get(f"Wg{par}", lambda hc=hc: np.ascontiguousarray(
                I["L1_attn_Wg"][:, hc])),
            "Wb": _get(f"Wb{par}", lambda hs=hs: np.ascontiguousarray(
                I["L1_attn_Wb"][:, hs])),
            "Wa": _get(f"Wa{par}", lambda hs=hs: np.ascontiguousarray(
                I["L1_attn_Wa"][:, hs])),
            "Alog": _get(f"Al{par}", lambda hs=hs: np.ascontiguousarray(
                I["L1_attn_A_log"][None, hs])),
            "dtb": _get(f"dt{par}", lambda hs=hs: np.ascontiguousarray(
                I["L1_attn_dt_bias"][None, hs])),
        }
    onw = _get("onw", lambda: np.ascontiguousarray(
        I["L1_attn_o_norm_w"][None, :].astype(np.float32)))
    shared = {"rw": I["L0_ffn_router_w"], "f0w1": I["L0_ffn_w1"],
              "f0b1": I["L0_ffn_b1"], "f0w2": I["L0_ffn_w2"],
              "f0b2": I["L0_ffn_b2"], "w_inT": w_inT, "w_outT": w_outT,
              "routers": I["L1_ffn_routers"], "f1w1": I["L1_ffn_W1"],
              "f1w2": I["L1_ffn_W2"], "onw": onw,
              "Wo": I["L1_attn_Wo"], "Wfc": I["L1_attn_W_fc"],
              "Wpj": I["L1_attn_W_proj"]}
    per_core = [dict(sl[c % 2], x=xs[c], posq=_POSQ[c], owntiles=_OWNT[c],
                     invq=_INVQ[c]) for c in range(8)]
    nc = _get("mega", build_mega)
    res = run8(nc, per_core, shared)
    xout = unshard_own([np.asarray(res[c]["out"]) for c in range(8)])
    return xout.reshape(B, S, D)
